# version 11
# baseline (speedup 1.0000x reference)
import numpy as np
import ml_dtypes
from contextlib import ExitStack

import concourse.bass as bass
import concourse.mybir as mybir
from concourse.bass_utils import run_bass_kernel_spmd

F32 = mybir.dt.float32
BF16 = mybir.dt.bfloat16
I8 = mybir.dt.int8
AF = mybir.ActivationFunctionType
ALU = mybir.AluOpType
AX = mybir.AxisListType

B, S, H = 8, 2048, 2048
NH, HD = 16, 128
BLK = 256
NBLK = S // BLK
DI = 8192
NHC = H // 128
NIC = DI // 128
TW = 512

N_SUB = 4
DEBUG_TAPS = False

NEG_BIG = -1.0e30


def _tile_lhsT_oc(Wnp):
    O, Hd = Wnp.shape
    a = Wnp.reshape(O // 128, 128, Hd // 128, 128)
    a = a.transpose(0, 3, 2, 1)
    return np.ascontiguousarray(a.reshape(O // 128, 128, Hd))


def _tile_lhsT_hc(Wnp):
    O, Hd = Wnp.shape
    a = Wnp.reshape(O // 128, 128, Hd // 128, 128)
    a = a.transpose(2, 3, 0, 1)
    return np.ascontiguousarray(a.reshape(Hd // 128, 128, O))


def _host_constants():
    c = {}
    c["ident_bf"] = np.eye(128, dtype=ml_dtypes.bfloat16)
    c["ident_f32"] = np.eye(128, dtype=np.float32)
    c["ones_row_f32"] = np.ones((1, 128), dtype=np.float32)
    c["ones_row_bf"] = np.ones((1, 128), dtype=ml_dtypes.bfloat16)
    c["ones_col_f32"] = np.ones((128, 1), dtype=np.float32)
    c["ones_col_bf"] = np.ones((128, 1), dtype=ml_dtypes.bfloat16)
    m = np.arange(BLK, dtype=np.float32)
    diffT = np.empty((2, 128, BLK), dtype=np.float32)
    for nh in range(2):
        n_abs = nh * 128 + np.arange(128, dtype=np.float32)
        d = m[None, :] - n_abs[:, None]
        d[d < 0] = 1.0e9
        diffT[nh] = d
    c["diffT"] = diffT
    c["qd_row"] = np.arange(1, BLK + 1, dtype=np.float32).reshape(1, BLK)
    kd = np.empty((128, 2), dtype=np.float32)
    for nh in range(2):
        kd[:, nh] = 255.0 - (nh * 128 + np.arange(128, dtype=np.float32))
    c["kd_col"] = kd
    ii = np.arange(128)
    tri = np.where(ii[None, :] <= ii[:, None], 0.0, NEG_BIG).astype(np.float32)
    c["tri_diag"] = tri
    i512 = np.arange(TW)
    m01 = np.zeros((4, 128, TW), dtype=np.float32)
    for k in range(4):
        m01[k] = (ii[:, None] + 128 * k <= i512[None, :]).astype(np.float32)
    c["m01"] = m01.astype(ml_dtypes.bfloat16)
    return c


def _rotary_tables():
    dim = HD
    sw = dim ** (-0.5)
    i = np.arange(0, dim, 2, dtype=np.float32)
    inv_freq = 1.0 / (10000.0 ** (i / dim))
    t = np.arange(S, dtype=np.float32)
    freqs = np.concatenate([np.outer(t, inv_freq)] * 2, axis=-1)
    scale_base = (i + 0.4 * dim) / (1.4 * dim)
    power = (t - S // 2) / 512.0
    xs = scale_base[None, :] ** power[:, None]
    xs = np.concatenate([xs, xs], axis=-1).astype(np.float32)
    cos = np.cos(freqs).astype(np.float32)
    sin = np.sin(freqs).astype(np.float32)

    def fold(scale):
        C = (sw * cos * scale).astype(np.float32)
        Sr = (sw * sin * scale).astype(np.float32)
        Sf = np.empty_like(Sr)
        Sf[:, :64] = -Sr[:, :64]
        Sf[:, 64:] = Sr[:, 64:]
        return np.ascontiguousarray(C.T), np.ascontiguousarray(Sf.T)

    cq, sq = fold(xs)
    ck, sk = fold((1.0 / xs).astype(np.float32))
    return {"rot_cq": cq, "rot_sq": sq, "rot_ck": ck, "rot_sk": sk}



class _Eng:
    def __init__(self, attr):
        self.attr = attr
        self.ops = []
        self.count = 0
        self.sem = None
        self.dsem = None
        self.issued = 0
        self.pending = 0
        self.memo = {}
        self.pending_barrier = None


class KB:
    ENG = {"pe": "tensor", "dve": "vector", "act": "scalar",
           "qs": "sync", "qg": "gpsimd"}

    def __init__(self, nc, block, ctx):
        self.nc = nc
        self.block = block
        self.e = {n: _Eng(a) for n, a in self.ENG.items()}
        for n, en in self.e.items():
            en.sem = ctx.enter_context(nc.semaphore(f"sem_{n}"))
            if n in ("qs", "qg"):
                en.dsem = ctx.enter_context(nc.semaphore(f"dsem_{n}"))

    def _flush_barrier(self, name):
        en = self.e[name]
        if en.pending_barrier is not None:
            snap = en.pending_barrier
            en.pending_barrier = None
            for other, val in snap.items():
                if other != name and val > 0:
                    self.wait(name, other, val)

    def emit(self, name, fn, inc=0):
        en = self.e[name]
        self._flush_barrier(name)
        tok = None
        if inc:
            en.count += inc
            tok = en.count
            sem = en.sem

            def wrapped(e, fn=fn, sem=sem, inc=inc):
                fn(e).then_inc(sem, inc)
            en.ops.append(wrapped)
        else:
            en.ops.append(fn)
        return tok

    def wait(self, name, producer, val):
        if val is None or val <= 0:
            return
        en = self.e[name]
        self._flush_barrier(name)
        assert val <= self.e[producer].count, (
            f"wait on future token {producer}={val} > {self.e[producer].count}")
        if en.memo.get(producer, 0) >= val:
            return
        en.memo[producer] = val
        sem = self.e[producer].sem
        en.ops.append(lambda e, sem=sem, val=val: e.wait_ge(sem, val))

    def dma(self, qname, out, in_, ckpt=True, **kw):
        en = self.e[qname]
        self._flush_barrier(qname)
        en.issued += 1
        en.pending += 1
        dsem = en.dsem

        def op(e, out=out, in_=in_, kw=kw, dsem=dsem):
            e.dma_start(out=out, in_=in_, **kw).then_inc(dsem, 16)
        en.ops.append(op)
        if ckpt:
            return self.ckpt(qname)
        return None

    def ckpt(self, qname):
        en = self.e[qname]
        if en.pending == 0:
            return en.count
        en.pending = 0
        en.count += 1
        tok = en.count
        target = 16 * en.issued
        dsem, csem = en.dsem, en.sem
        en.ops.append(lambda e, dsem=dsem, target=target, csem=csem:
                      e.wait_ge(dsem, target).then_inc(csem, 1))
        return tok

    def barrier(self):
        snap = {n: en.count for n, en in self.e.items()}
        for en in self.e.values():
            en.pending_barrier = dict(snap)

    def flush(self):
        for n, en in self.e.items():
            if not en.ops:
                continue
            ops = en.ops
            en.ops = []
            getattr(self.block, en.attr)(lambda eng, ops=ops: [f(eng) for f in ops])

    def tok(self, name):
        return self.e[name].count



def build_program(n_sub=N_SUB):
    nc = bass.Bass("TRN2", target_bir_lowering=False, debug=False)
    dt_in = {}

    def din(name, shape, dtype):
        dt_in[name] = nc.dram_tensor(name, list(shape), dtype, kind="ExternalInput").ap()
        return dt_in[name]

    xT = din("xT", [NHC, 128, S], F32)
    wqkvT = din("wqkvT", [48, 128, H], F32)
    wgateT = din("wgateT", [16, 128, H], F32)
    woutT = din("woutT", [16, 128, H], F32)
    wqT = din("wqT", [16, 128, H], F32)
    wkvT = din("wkvT", [2, 128, H], F32)
    wprojT = din("wprojT", [16, 128, H], F32)
    din("slope", [1, NH], F32)
    din("normw", [128, NHC], F32)
    din("bproj", [128, NHC], F32)
    ffn8 = din("ffn8", [NIC, 2, 128, H], I8)
    ffn8d = din("ffn8d", [NIC, 128, H], I8)
    din("gs_t", [128, NIC], F32)
    din("us_t", [128, NIC], F32)
    din("ds_t", [128, NHC], F32)
    consts = _host_constants()
    for nm, arr in consts.items():
        din(nm, arr.shape, F32 if arr.dtype == np.dtype(np.float32) else BF16)
    for nm in ("rot_cq", "rot_sq", "rot_ck", "rot_sk"):
        din(nm, [128, S], F32)

    yT = nc.dram_tensor("yT", [NHC, 128, S], F32, kind="ExternalOutput").ap()
    x_res = nc.dram_tensor("x_res", [NHC, 128, S], F32, kind="Internal").ap()
    qkv_scr = nc.dram_tensor("qkv_scr", [NBLK, 128, 48 * BLK], BF16, kind="Internal").ap()
    gate_scr = nc.dram_tensor("gate_scr", [NBLK, 128, 16 * BLK], BF16, kind="Internal").ap()
    gated_scr = nc.dram_tensor("gated_scr", [NBLK, 128, 16 * BLK], BF16,
                               kind="Internal").ap()
    q_scr = nc.dram_tensor("q_scr", [S // TW, 128, 16 * TW], BF16, kind="Internal").ap()

    with ExitStack() as top:
        block = top.enter_context(nc.Block())
        K = KB(nc, block, top)

        def alloc(ctx, name, shape, dtype):
            return ctx.enter_context(nc.sbuf_tensor(name, list(shape), dtype))

        def pbank(ctx, name, dtype=F32):
            return ctx.enter_context(nc.psum_tensor(
                name, [128, 512 if dtype == F32 else 1024], dtype))

        ident = alloc(top, "sb_ident", [128, 128], BF16)
        ident32 = alloc(top, "sb_ident32", [128, 128], F32)
        ones_row_f32 = alloc(top, "sb_ones_row_f32", [1, 128], F32)
        ones_row_bf = alloc(top, "sb_ones_row_bf", [1, 128], BF16)
        ones_col_f32 = alloc(top, "sb_ones_col_f32", [128, 1], F32)
        ones_col_bf = alloc(top, "sb_ones_col_bf", [128, 1], BF16)
        s_normw = alloc(top, "s_normw", [128, NHC], F32)
        s_bproj = alloc(top, "s_bproj", [128, NHC], F32)
        s_gs = alloc(top, "s_gs", [128, NIC], F32)
        s_us = alloc(top, "s_us", [128, NIC], F32)
        s_ds = alloc(top, "s_ds", [128, NHC], F32)
        s_tri = alloc(top, "s_tri", [128, 128], F32)
        s_m01 = alloc(top, "s_m01", [128, 4 * TW], BF16)

        for nm, t in [("ident_bf", ident), ("ident_f32", ident32),
                      ("ones_row_f32", ones_row_f32), ("ones_row_bf", ones_row_bf),
                      ("ones_col_f32", ones_col_f32), ("ones_col_bf", ones_col_bf),
                      ("normw", s_normw), ("bproj", s_bproj), ("gs_t", s_gs),
                      ("us_t", s_us), ("ds_t", s_ds), ("tri_diag", s_tri)]:
            K.dma("qs", t[:, :], dt_in[nm][:, :])
        K.dma("qs", s_m01.ap().rearrange("p (k f) -> p k f", k=4),
              dt_in["m01"].rearrange("k p f -> p k f"))
        tok_consts = K.tok("qs")

        def emit_ln(x_tok, xb, xn_dst, scr, bk_mu, bk_rs, bk_r1, bk_r2, eps=1e-5):
            tw = TW
            K.wait("act", "qs", x_tok)
            K.wait("dve", "qs", x_tok)
            K.wait("dve", "pe", scr.get("_t_rows", 0))
            xv = xb.ap().rearrange("p (hc t) -> p t hc", hc=NHC)
            K.emit("dve", lambda e, xv=xv: e.tensor_reduce(
                scr["acc"][:, :], xv, AX.X, ALU.add), inc=1)
            for hc in range(NHC):
                sq = scr["sq"][:, (hc % 2) * tw:(hc % 2 + 1) * tw]
                if hc >= 2:
                    K.wait("act", "dve", scr["_sqtok"][hc - 2])
                elif (NHC - 2 + hc) in scr["_sqtok"]:
                    K.wait("act", "dve", scr["_sqtok"][NHC - 2 + hc])
                K.emit("act", lambda e, hc=hc, sq=sq: e.activation(
                    sq, xb[:, hc * tw:(hc + 1) * tw], AF.Square), inc=1)
                K.wait("dve", "act", K.tok("act"))
                if hc == 0:
                    K.emit("dve", lambda e, sq=sq: e.tensor_copy(scr["acc2"][:, :], sq),
                           inc=1)
                else:
                    K.emit("dve", lambda e, sq=sq: e.tensor_add(
                        scr["acc2"][:, :], scr["acc2"][:, :], sq), inc=1)
                scr["_sqtok"][hc] = K.tok("dve")
            t_acc = K.tok("dve")
            K.wait("pe", "dve", t_acc)
            K.emit("pe", lambda e: e.matmul(bk_r1[0:1, :], ones_col_f32[:, :],
                                            scr["acc"][:, :], start=True, stop=True))
            K.emit("pe", lambda e: e.matmul(bk_r2[0:1, :], ones_col_f32[:, :],
                                            scr["acc2"][:, :], start=True, stop=True),
                   inc=1)
            t_rows = K.tok("pe")
            scr["_t_rows"] = t_rows
            K.wait("dve", "pe", t_rows)
            K.wait("dve", "act", scr.get("_t_rstd", 0))
            r_nm, r_ss, r_t = scr["row_a"], scr["row_b"], scr["row_c"]
            K.emit("dve", lambda e: e.tensor_scalar_mul(r_nm[0:1, :], bk_r1[0:1, :],
                                                        -1.0 / H))
            K.emit("dve", lambda e: e.tensor_scalar_mul(r_ss[0:1, :], bk_r2[0:1, :],
                                                        1.0 / H))
            K.emit("dve", lambda e: e.tensor_mul(r_t[0:1, :], r_nm[0:1, :], r_nm[0:1, :]))
            K.emit("dve", lambda e, eps=eps: e.scalar_tensor_tensor(
                r_ss[0:1, :], r_ss[0:1, :], float(eps), r_t[0:1, :],
                ALU.add, ALU.subtract), inc=1)
            t_var = K.tok("dve")
            K.wait("act", "dve", t_var)
            K.emit("act", lambda e: e.activation(r_ss[0:1, :], r_ss[0:1, :], AF.Ln))
            K.emit("act", lambda e: e.activation(r_ss[0:1, :], r_ss[0:1, :],
                                                 AF.Exp, scale=-0.5), inc=1)
            t_rstd = K.tok("act")
            scr["_t_rstd"] = t_rstd
            K.wait("pe", "dve", t_var)
            K.wait("pe", "act", t_rstd)
            K.emit("pe", lambda e: e.matmul(bk_mu[:, :], ones_row_f32[:, :],
                                            r_nm[0:1, :], start=True, stop=True))
            K.emit("pe", lambda e: e.matmul(bk_rs[:, :], ones_row_f32[:, :],
                                            r_ss[0:1, :], start=True, stop=True), inc=1)
            t_bc = K.tok("pe")
            K.wait("dve", "pe", t_bc)
            K.emit("dve", lambda e: e.tensor_copy(scr["mu_b"][:, :], bk_mu[:, :]))
            K.emit("dve", lambda e: e.tensor_copy(scr["rs_b"][:, :], bk_rs[:, :]), inc=1)
            for hc in range(NHC):
                K.emit("dve", lambda e, hc=hc: e.tensor_add(
                    scr["t1"][:, :], xb[:, hc * tw:(hc + 1) * tw], scr["mu_b"][:, :]))
                K.emit("dve", lambda e, hc=hc: e.tensor_mul(
                    xn_dst(hc), scr["t1"][:, :], scr["rs_b"][:, :]),
                    inc=(1 if hc == NHC - 1 else 0))
            return K.tok("dve")

        def ln_scratch(ctx, tag):
            return {
                "sq": alloc(ctx, f"{tag}_sq", [128, 2 * TW], F32),
                "acc": alloc(ctx, f"{tag}_acc", [128, TW], F32),
                "acc2": alloc(ctx, f"{tag}_acc2", [128, TW], F32),
                "t1": alloc(ctx, f"{tag}_t1", [128, TW], F32),
                "mu_b": alloc(ctx, f"{tag}_mub", [128, TW], F32),
                "rs_b": alloc(ctx, f"{tag}_rsb", [128, TW], F32),
                "row_a": alloc(ctx, f"{tag}_ra", [1, TW], F32),
                "row_b": alloc(ctx, f"{tag}_rb", [1, TW], F32),
                "row_c": alloc(ctx, f"{tag}_rc", [1, TW], F32),
                "_sqtok": {},
            }

        def load_x_chunk(src, t0, xb, tw=TW):
            return K.dma("qs", xb.ap().rearrange("p (hc t) -> p hc t", hc=NHC),
                         src[:, :, t0:t0 + tw].rearrange("hc p t -> p hc t"))

        def store_x_chunk(dst, t0, xb, tw=TW):
            return K.dma("qs", dst[:, :, t0:t0 + tw].rearrange("hc p t -> p hc t"),
                         xb.ap().rearrange("p (hc t) -> p hc t", hc=NHC))

        def phase_a():
            with ExitStack() as pha:
                s_diffT = alloc(pha, "s_diffT", [128, 2 * BLK], F32)
                s_qdrow = alloc(pha, "s_qdrow", [1, BLK], F32)
                s_kdcol = alloc(pha, "s_kdcol", [128, 2], F32)
                s_srow = alloc(pha, "s_srow", [1, NH], F32)
                s_negs = alloc(pha, "s_negs", [128, NH], F32)
                s_bd = alloc(pha, "s_bd", [128, NH], F32)
                maskT = alloc(pha, "maskT", [128, NH * 2 * BLK], BF16)
                qdec = alloc(pha, "qdec", [128, NH * BLK], BF16)
                kdec = alloc(pha, "kdec", [128, 2 * NH], F32)
                kv32 = alloc(pha, "kv32", [128, NH * HD], F32)
                kvbf = alloc(pha, "kvbf", [128, NH * HD], BF16)
                tmpA = alloc(pha, "tmpA", [128, 2 * BLK], F32)
                tmpB = alloc(pha, "tmpB", [128, 4], F32)
                rowA = alloc(pha, "rowA", [1, BLK], F32)
                rowB = alloc(pha, "rowB", [1, NH], F32)

                with ExitStack() as ph0:
                    p_set = pbank(ph0, "pA_set")
                    K.dma("qs", s_diffT.ap().rearrange("p (n f) -> p n f", n=2),
                          dt_in["diffT"].rearrange("n p f -> p n f"))
                    K.dma("qs", s_qdrow[:, :], dt_in["qd_row"][:, :])
                    K.dma("qs", s_kdcol[:, :], dt_in["kd_col"][:, :])
                    t_sl = K.dma("qs", s_srow[:, :], dt_in["slope"][:, :])
                    K.wait("dve", "qs", t_sl)
                    K.wait("pe", "qs", tok_consts)
                    K.emit("dve", lambda e: e.tensor_scalar_mul(rowB[0:1, :], s_srow[0:1, :],
                                                                -1.0), inc=1)
                    K.wait("pe", "dve", K.tok("dve"))
                    K.emit("pe", lambda e: e.matmul(p_set[:, 0:NH], ones_row_f32[:, :],
                                                    rowB[0:1, :], start=True, stop=True),
                           inc=1)
                    K.wait("dve", "pe", K.tok("pe"))
                    K.emit("dve", lambda e: e.tensor_copy(s_negs[:, :], p_set[:, 0:NH]),
                           inc=1)
                    K.emit("dve", lambda e: e.tensor_scalar_mul(rowB[0:1, :], s_srow[0:1, :],
                                                                -float(BLK)), inc=1)
                    K.wait("act", "dve", K.tok("dve"))
                    K.emit("act", lambda e: e.activation(rowB[0:1, :], rowB[0:1, :], AF.Exp),
                           inc=1)
                    K.wait("pe", "act", K.tok("act"))
                    K.wait("pe", "dve", K.tok("dve"))
                    K.emit("pe", lambda e: e.matmul(p_set[:, 0:NH], ones_row_f32[:, :],
                                                    rowB[0:1, :], start=True, stop=True),
                           inc=1)
                    K.wait("dve", "pe", K.tok("pe"))
                    K.emit("dve", lambda e: e.tensor_copy(s_bd[:, :], p_set[:, 0:NH]), inc=1)
                    K.barrier()
                    for h in range(NH):
                        for nh in range(2):
                            K.emit("dve", lambda e, h=h, nh=nh: e.tensor_scalar(
                                tmpA[:, nh * BLK:(nh + 1) * BLK],
                                s_diffT[:, nh * BLK:(nh + 1) * BLK],
                                s_negs[:, h:h + 1], None, ALU.mult), inc=(nh == 1))
                        K.wait("act", "dve", K.tok("dve"))
                        for nh in range(2):
                            K.emit("act", lambda e, h=h, nh=nh: e.activation(
                                maskT[:, (2 * h + nh) * BLK:(2 * h + nh + 1) * BLK],
                                tmpA[:, nh * BLK:(nh + 1) * BLK], AF.Exp), inc=(nh == 1))
                        t_mask = K.tok("act")
                        K.emit("dve", lambda e, h=h: e.tensor_scalar(
                            tmpB[:, 0:2], s_kdcol[:, :],
                            s_negs[:, h:h + 1], None, ALU.mult), inc=1)
                        K.wait("act", "dve", K.tok("dve"))
                        K.emit("act", lambda e, h=h: e.activation(
                            kdec[:, 2 * h:2 * h + 2], tmpB[:, 0:2], AF.Exp),
                            inc=1)
                        K.wait("dve", "act", t_mask)
                        K.emit("dve", lambda e, h=h: e.tensor_scalar(
                            rowA[0:1, :], s_qdrow[0:1, :], s_negs[0:1, h:h + 1], None,
                            ALU.mult), inc=1)
                        K.wait("act", "dve", K.tok("dve"))
                        K.emit("act", lambda e: e.activation(rowA[0:1, :], rowA[0:1, :],
                                                             AF.Exp), inc=1)
                        K.wait("pe", "act", K.tok("act"))
                        K.emit("pe", lambda e: e.matmul(p_set[:, 0:BLK], ones_row_f32[:, :],
                                                        rowA[0:1, :], start=True, stop=True),
                               inc=1)
                        K.wait("dve", "pe", K.tok("pe"))
                        K.emit("dve", lambda e, h=h: e.tensor_copy(
                            qdec[:, h * BLK:(h + 1) * BLK], p_set[:, 0:BLK]), inc=1)
                    K.emit("dve", lambda e: e.memset(kv32[:, :], 0.0))
                    K.emit("dve", lambda e: e.memset(kvbf[:, :], 0.0), inc=1)
                    K.barrier()

                with ExitStack() as a1:
                    xb = alloc(a1, "a1_xb", [128, NHC * TW], F32)
                    xn = alloc(a1, "a1_xn", [128, NHC * S], BF16)
                    scr = ln_scratch(a1, "a1")
                    wbuf = [alloc(a1, f"a1_w{i}", [128, H], BF16) for i in range(2)]
                    qgbuf = [alloc(a1, f"a1_qg{i}", [128, S], BF16) for i in range(2)]
                    pq = [pbank(a1, f"pA1_{i}") for i in range(2)]
                    bmu = pbank(a1, "pA1_mu")
                    brs = pbank(a1, "pA1_rs")
                    br1 = pbank(a1, "pA1_r1")
                    br2 = pbank(a1, "pA1_r2")

                    ln_prev = 0
                    for tc in range(S // TW):
                        t0 = tc * TW
                        K.wait("qs", "dve", ln_prev)
                        tld = load_x_chunk(xT, t0, xb)
                        ln_prev = emit_ln(
                            tld, xb,
                            lambda hc, t0=t0: xn[:, hc * S + t0:hc * S + t0 + TW],
                            scr, bmu, brs, br1, br2)
                    xn_tok = K.tok("dve")

                    w_tok = [0, 0]
                    pe_w = [0, 0]
                    ev_p = [0, 0]
                    st_tok = [0, 0]
                    K.wait("pe", "dve", xn_tok)
                    for j in range(64):
                        is_gate = j >= 48
                        oc = j - 48 if is_gate else j
                        sl = j % 2
                        src = (wgateT if is_gate else wqkvT)[oc]
                        K.wait("qg", "pe", pe_w[sl])
                        w_tok[sl] = K.dma("qg", wbuf[sl][:, :], src[:, :])
                        K.wait("pe", "qg", w_tok[sl])
                        for q in range(4):
                            psl = q % 2
                            K.wait("pe", "act", ev_p[psl])
                            for hc in range(NHC):
                                K.emit("pe", lambda e, sl=sl, q=q, hc=hc, psl=psl: e.matmul(
                                    pq[psl][:, :], wbuf[sl][:, hc * 128:(hc + 1) * 128],
                                    xn[:, hc * S + q * 512:hc * S + (q + 1) * 512],
                                    start=(hc == 0), stop=(hc == NHC - 1)),
                                    inc=(1 if hc == NHC - 1 else 0))
                            if q == 3:
                                pe_w[sl] = K.tok("pe")
                            K.wait("act", "pe", K.tok("pe"))
                            K.wait("act", "qs", st_tok[sl])
                            fn = AF.Sigmoid if is_gate else AF.Silu
                            K.emit("act", lambda e, sl=sl, q=q, fn=fn, psl=psl: e.activation(
                                qgbuf[sl][:, q * 512:(q + 1) * 512], pq[psl][:, :], fn),
                                inc=1)
                            ev_p[psl] = K.tok("act")
                        K.wait("qs", "act", K.tok("act"))
                        dst = (gate_scr if is_gate else qkv_scr)
                        st_tok[sl] = K.dma(
                            "qs",
                            dst[:, :, oc * BLK:(oc + 1) * BLK].rearrange(
                                "tb p f -> p tb f"),
                            qgbuf[sl].ap().rearrange("p (tb f) -> p tb f", tb=NBLK))
                    K.barrier()

                with ExitStack() as a2:
                    TB = BLK
                    qkvb = [alloc(a2, f"a2_qkv{i}", [128, 48 * TB], BF16) for i in range(2)]
                    gateb = [alloc(a2, f"a2_g{i}", [128, 16 * TB], BF16) for i in range(2)]
                    attnb = alloc(a2, "a2_attn", [128, NHC * TB], F32)
                    gatedb = alloc(a2, "a2_gated", [128, NHC * TB], BF16)
                    kT_s = [alloc(a2, f"a2_kT{i}", [128, 2 * 128], BF16) for i in range(2)]
                    vT_s = [alloc(a2, f"a2_vT{i}", [128, 2 * 128], BF16) for i in range(2)]
                    qs_s = [alloc(a2, f"a2_qs{i}", [128, TB], BF16) for i in range(2)]
                    pT_s = [alloc(a2, f"a2_pT{i}", [128, 2 * TB], BF16) for i in range(2)]
                    sq_s = alloc(a2, "a2_sq", [128, 2 * TB], F32)
                    rms_acc = alloc(a2, "a2_racc", [128, TB], F32)
                    gt_s = alloc(a2, "a2_gt", [128, TB], F32)
                    rs_b = alloc(a2, "a2_rsb", [128, TB], F32)
                    row_r = alloc(a2, "a2_rowr", [1, TB], F32)

                    b_sc = [pbank(a2, f"pA2_sc{i}") for i in range(2)]
                    b_o = [pbank(a2, f"pA2_o{i}") for i in range(2)]
                    b_dkv = [pbank(a2, f"pA2_dkv{i}") for i in range(2)]
                    b_tr1 = pbank(a2, "pA2_tr1", BF16)
                    b_tr2 = pbank(a2, "pA2_tr2", BF16)

                    kvb_t = [0] * NH
                    tr1_c = [0, 0]
                    tr2_c = [0, 0]
                    sc_c = [0, 0]
                    o_c = [0, 0]
                    dkv_c = [0, 0]
                    kts_c = [0, 0]
                    vts_c = [0, 0]
                    qs_c = [0, 0]
                    pts_c = [0, 0]
                    qkvb_pe = [0, 0]
                    qkvb_dve = [0, 0]
                    gateb_c = [0, 0]
                    t_gated = 0
                    t_msq = 0
                    t_racc = 0
                    rowr_act = 0
                    rsb_c = 0
                    st_prev = 0
                    gated_st = []
                    for tb in range(NBLK):
                        sl = tb % 2
                        t0 = tb * TB
                        K.wait("qg", "pe", qkvb_pe[sl])
                        K.wait("qg", "dve", qkvb_dve[sl])
                        qkv_tok = K.dma("qg", qkvb[sl][:, :], qkv_scr[tb][:, :])
                        K.wait("qg", "dve", gateb_c[sl])
                        gate_tok = K.dma("qg", gateb[sl][:, :], gate_scr[tb][:, :])
                        K.wait("pe", "qg", qkv_tok)
                        K.wait("act", "dve", t_gated)
                        for h in range(NH):
                            hp = h % 2
                            qof = (3 * h) * TB
                            kof = (3 * h + 1) * TB
                            vof = (3 * h + 2) * TB
                            tro = hp * 256
                            K.wait("pe", "dve", tr1_c[hp])
                            for c in range(2):
                                K.emit("pe", lambda e, sl=sl, kof=kof, c=c, tro=tro:
                                       e.transpose(
                                           b_tr1[:, tro + c * 128:tro + (c + 1) * 128],
                                           qkvb[sl][:, kof + c * 128:kof + (c + 1) * 128],
                                           ident[:, :]), inc=(c == 1))
                            t_ktr = K.tok("pe")
                            K.wait("pe", "act", tr2_c[hp])
                            for c in range(2):
                                K.emit("pe", lambda e, sl=sl, vof=vof, c=c, tro=tro:
                                       e.transpose(
                                           b_tr2[:, tro + c * 128:tro + (c + 1) * 128],
                                           qkvb[sl][:, vof + c * 128:vof + (c + 1) * 128],
                                           ident[:, :]), inc=(c == 1))
                            t_vtr = K.tok("pe")
                            K.wait("pe", "dve", sc_c[hp])
                            for c in range(2):
                                K.emit("pe", lambda e, sl=sl, kof=kof, qof=qof, c=c,
                                       hp=hp: e.matmul(
                                           b_sc[hp][:, c * 256:(c + 1) * 256],
                                           qkvb[sl][:, kof + c * 128:kof + (c + 1) * 128],
                                           qkvb[sl][:, qof:qof + TB],
                                           start=True, stop=True), inc=(c == 1))
                            t_sc = K.tok("pe")
                            qkvb_pe[sl] = t_sc
                            K.wait("dve", "pe", t_ktr)
                            K.wait("dve", "pe", kts_c[hp])
                            for c in range(2):
                                K.emit("dve", lambda e, c=c, h=h, hp=hp: e.tensor_scalar(
                                    kT_s[hp][:, c * 128:(c + 1) * 128],
                                    b_tr1[:, hp * 256 + c * 128:hp * 256 + (c + 1) * 128],
                                    kdec[:, 2 * h + c:2 * h + c + 1], None, ALU.mult),
                                    inc=(c == 1))
                            tr1_c[hp] = K.tok("dve")
                            K.wait("act", "pe", t_vtr)
                            K.wait("act", "pe", vts_c[hp])
                            for c in range(2):
                                K.emit("act", lambda e, c=c, hp=hp: e.copy(
                                    vT_s[hp][:, c * 128:(c + 1) * 128],
                                    b_tr2[:, hp * 256 + c * 128:hp * 256 + (c + 1) * 128]),
                                    inc=(c == 1))
                            t_vt = K.tok("act")
                            tr2_c[hp] = t_vt
                            K.wait("dve", "pe", qs_c[hp])
                            K.emit("dve", lambda e, sl=sl, qof=qof, h=h, hp=hp:
                                   e.tensor_mul(
                                       qs_s[hp][:, :], qkvb[sl][:, qof:qof + TB],
                                       qdec[:, h * BLK:(h + 1) * BLK]), inc=1)
                            t_qs = K.tok("dve")
                            qkvb_dve[sl] = t_qs
                            K.wait("dve", "pe", t_sc)
                            K.wait("dve", "pe", pts_c[hp])
                            for c in range(2):
                                K.emit("dve", lambda e, c=c, h=h, hp=hp: e.tensor_mul(
                                    pT_s[hp][:, c * TB:(c + 1) * TB],
                                    b_sc[hp][:, c * 256:(c + 1) * 256],
                                    maskT[:, (2 * h + c) * BLK:(2 * h + c + 1) * BLK]),
                                    inc=(c == 1))
                            t_pT = K.tok("dve")
                            sc_c[hp] = t_pT
                            K.wait("pe", "act", o_c[hp])
                            K.wait("pe", "dve", t_qs)
                            K.wait("pe", "act", kvb_t[h])
                            K.emit("pe", lambda e, h=h, hp=hp: e.matmul(
                                b_o[hp][:, 0:256], kvbf[:, h * HD:(h + 1) * HD],
                                qs_s[hp][:, :], start=True, stop=False))
                            K.wait("pe", "act", t_vt)
                            K.wait("pe", "dve", t_pT)
                            for c in range(2):
                                K.emit("pe", lambda e, c=c, hp=hp: e.matmul(
                                    b_o[hp][:, 0:256], vT_s[hp][:, c * 128:(c + 1) * 128],
                                    pT_s[hp][:, c * TB:(c + 1) * TB],
                                    start=False, stop=(c == 1)), inc=(c == 1))
                            t_o = K.tok("pe")
                            qs_c[hp] = t_o
                            pts_c[hp] = t_o
                            K.wait("pe", "dve", dkv_c[hp])
                            for c in range(2):
                                K.emit("pe", lambda e, c=c, hp=hp: e.matmul(
                                    b_dkv[hp][:, 0:128], kT_s[hp][:, c * 128:(c + 1) * 128],
                                    vT_s[hp][:, c * 128:(c + 1) * 128],
                                    start=(c == 0), stop=(c == 1)), inc=(c == 1))
                            t_dkv = K.tok("pe")
                            kts_c[hp] = t_dkv
                            vts_c[hp] = t_dkv
                            K.wait("act", "pe", t_o)
                            K.emit("act", lambda e, h=h, hp=hp: e.copy(
                                attnb[:, h * TB:(h + 1) * TB], b_o[hp][:, 0:256]), inc=1)
                            o_c[hp] = K.tok("act")
                            K.wait("dve", "pe", t_dkv)
                            K.emit("dve", lambda e, h=h, hp=hp: e.scalar_tensor_tensor(
                                kv32[:, h * HD:(h + 1) * HD], kv32[:, h * HD:(h + 1) * HD],
                                s_bd[:, h:h + 1], b_dkv[hp][:, 0:128], ALU.mult, ALU.add),
                                inc=1)
                            dkv_c[hp] = K.tok("dve")
                            K.wait("act", "dve", dkv_c[hp])
                            K.emit("act", lambda e, h=h: e.copy(
                                kvbf[:, h * HD:(h + 1) * HD],
                                kv32[:, h * HD:(h + 1) * HD]), inc=1)
                            kvb_t[h] = K.tok("act")
                        t_attn = K.tok("act")
                        K.wait("act", "dve", t_racc)
                        sqd = {}
                        for hc in range(NHC):
                            ssl = hc % 2
                            K.wait("act", "dve", sqd.get(hc - 2, 0))
                            K.emit("act", lambda e, hc=hc, ssl=ssl: e.activation(
                                sq_s[:, ssl * TB:(ssl + 1) * TB],
                                attnb[:, hc * TB:(hc + 1) * TB], AF.Square), inc=1)
                            K.wait("dve", "act", K.tok("act"))
                            if hc == 0:
                                K.wait("dve", "pe", t_msq)
                                K.emit("dve", lambda e, ssl=ssl: e.tensor_copy(
                                    rms_acc[:, :], sq_s[:, ssl * TB:(ssl + 1) * TB]),
                                    inc=1)
                            else:
                                K.emit("dve", lambda e, ssl=ssl: e.tensor_add(
                                    rms_acc[:, :], rms_acc[:, :],
                                    sq_s[:, ssl * TB:(ssl + 1) * TB]), inc=1)
                            sqd[hc] = K.tok("dve")
                        t_racc = K.tok("dve")
                        K.wait("pe", "dve", t_racc)
                        K.wait("pe", "dve", dkv_c[0])
                        K.emit("pe", lambda e: e.matmul(
                            b_dkv[0][0:1, 128:128 + TB], ones_col_f32[:, :],
                            rms_acc[:, :], start=True, stop=True), inc=1)
                        t_msq = K.tok("pe")
                        K.wait("dve", "pe", t_msq)
                        K.wait("dve", "act", rowr_act)
                        K.emit("dve", lambda e: e.tensor_scalar(
                            row_r[0:1, :], b_dkv[0][0:1, 128:128 + TB], 1.0 / H, 1e-6,
                            ALU.mult, ALU.add), inc=1)
                        K.wait("act", "dve", K.tok("dve"))
                        K.emit("act", lambda e: e.activation(row_r[0:1, :], row_r[0:1, :],
                                                             AF.Ln))
                        K.emit("act", lambda e: e.activation(row_r[0:1, :], row_r[0:1, :],
                                                             AF.Exp, scale=-0.5), inc=1)
                        rowr_act = K.tok("act")
                        K.wait("pe", "act", rowr_act)
                        K.wait("pe", "act", o_c[0])
                        K.wait("pe", "dve", rsb_c)
                        K.emit("pe", lambda e: e.matmul(
                            b_o[0][:, 256:512], ones_row_f32[:, :], row_r[0:1, :],
                            start=True, stop=True), inc=1)
                        K.wait("dve", "pe", K.tok("pe"))
                        K.emit("dve", lambda e: e.tensor_copy(rs_b[:, :],
                                                              b_o[0][:, 256:512]),
                               inc=1)
                        rsb_c = K.tok("dve")
                        K.wait("dve", "qg", gate_tok)
                        K.wait("dve", "act", t_attn)
                        K.wait("dve", "qs", st_prev)
                        for hc in range(NHC):
                            K.emit("dve", lambda e, hc=hc: e.scalar_tensor_tensor(
                                gt_s[:, :], attnb[:, hc * TB:(hc + 1) * TB],
                                s_normw[:, hc:hc + 1], rs_b[:, :], ALU.mult, ALU.mult))
                            K.emit("dve", lambda e, hc=hc, sl=sl: e.tensor_mul(
                                gatedb[:, hc * TB:(hc + 1) * TB], gt_s[:, :],
                                gateb[sl][:, hc * TB:(hc + 1) * TB]),
                                inc=(1 if hc == NHC - 1 else 0))
                        t_gated = K.tok("dve")
                        gateb_c[sl] = t_gated
                        K.wait("qs", "dve", t_gated)
                        st_prev = K.dma("qs", gated_scr[tb][:, :], gatedb[:, :])
                        gated_st.append(st_prev)

                K.barrier()
                with ExitStack() as a2b:
                    TB = BLK
                    wout_all = alloc(a2b, "a2b_wo", [128, NHC * H], BF16)
                    gb = [alloc(a2b, f"a2b_g{i}", [128, NHC * TB], BF16)
                          for i in range(2)]
                    xq = [alloc(a2b, f"a2b_x{i}", [128, NHC * TB], F32)
                          for i in range(2)]
                    yb = [alloc(a2b, f"a2b_y{i}", [128, TB], F32) for i in range(2)]
                    b_y = [pbank(a2b, f"pA2b_{i}") for i in range(3)]

                    for fc in range(NHC):
                        K.dma("qg", wout_all[:, fc * H:(fc + 1) * H], woutT[fc][:, :],
                              ckpt=(fc == NHC - 1))
                    wo_tok = K.tok("qg")
                    K.wait("pe", "dve", K.tok("dve"))
                    K.wait("pe", "act", K.tok("act"))
                    gb_c = [0, 0]
                    yb_st = [0, 0]
                    ybank_c = [0, 0, 0]
                    xq_c = [0, 0]
                    rr2 = 0
                    for tb in range(NBLK):
                        s2 = tb % 2
                        K.wait("qg", "pe", gb_c[s2])
                        K.wait("qg", "qs", gated_st[tb])
                        g_tok = K.dma("qg", gb[s2][:, :], gated_scr[tb][:, :])
                        K.wait("qs", "dve", xq_c[s2])
                        x_tok = load_x_chunk(xT, tb * TB, xq[s2], TB)
                        K.wait("pe", "qg", g_tok)
                        if tb == 0:
                            K.wait("pe", "qg", wo_tok)
                        for fc in range(NHC):
                            bi = rr2 % 3
                            rr2 += 1
                            bk = b_y[bi]
                            K.wait("pe", "dve", ybank_c[bi])
                            for oc in range(NHC):
                                K.emit("pe", lambda e, fc=fc, oc=oc, bk=bk, s2=s2:
                                       e.matmul(
                                           bk[:, 0:TB],
                                           wout_all[:, fc * H + oc * 128:
                                                    fc * H + (oc + 1) * 128],
                                           gb[s2][:, oc * TB:(oc + 1) * TB],
                                           start=(oc == 0), stop=(oc == NHC - 1)),
                                       inc=(1 if oc == NHC - 1 else 0))
                            t_y = K.tok("pe")
                            ysl = fc % 2
                            K.wait("dve", "pe", t_y)
                            K.wait("dve", "qs", x_tok)
                            K.wait("dve", "qs", yb_st[ysl])
                            K.emit("dve", lambda e, fc=fc, bk=bk, ysl=ysl, s2=s2:
                                   e.tensor_add(
                                       yb[ysl][:, :], bk[:, 0:TB],
                                       xq[s2][:, fc * TB:(fc + 1) * TB]), inc=1)
                            ybank_c[bi] = K.tok("dve")
                            xq_c[s2] = ybank_c[bi]
                            K.wait("qs", "dve", ybank_c[bi])
                            yb_st[ysl] = K.dma(
                                "qs", x_res[fc][:, tb * TB:(tb + 1) * TB],
                                yb[ysl][:, :])
                        gb_c[s2] = K.tok("pe")
                    K.barrier()

        if n_sub >= 1:
            phase_a()


        def ffn_phase(tag, src, dst):
            GRP = 8
            NG = NIC // GRP
            NCH = S // TW
            with ExitStack() as fb:
                xb2 = [alloc(fb, f"{tag}_xb{i}", [128, NHC * TW], F32) for i in range(2)]
                xnf2 = [alloc(fb, f"{tag}_xn{i}", [128, NHC * TW], BF16) for i in range(2)]
                scr = ln_scratch(fb, tag)
                wbuf = [alloc(fb, f"{tag}_w{i}", [128, 2 * H], BF16) for i in range(2)]
                wd_grp = alloc(fb, f"{tag}_wd", [128, GRP * H], BF16)
                act_g = [alloc(fb, f"{tag}_ag{i}", [128, GRP * TW], BF16) for i in range(2)]
                silu_s = alloc(fb, f"{tag}_silu", [128, TW], F32)
                b_g = pbank(fb, f"p{tag}_g")
                b_u = pbank(fb, f"p{tag}_u")
                b_d = [pbank(fb, f"p{tag}_d{i}") for i in range(2)]
                bmu = pbank(fb, f"p{tag}_mu")
                brs = pbank(fb, f"p{tag}_rs")
                br1 = pbank(fb, f"p{tag}_r1")
                br2 = pbank(fb, f"p{tag}_r2")

                x_tok = [None] * (NCH + 1)
                ln_tok = [None] * (NCH + 1)
                x_tok[0] = load_x_chunk(src, 0, xb2[0])
                ln_tok[0] = emit_ln(
                    x_tok[0], xb2[0],
                    lambda hc: xnf2[0][:, hc * TW:(hc + 1) * TW],
                    scr, bmu, brs, br1, br2)
                w_tok = [0, 0]
                pe_w = [0, 0]
                t_silu = 0
                t_actg = [0, 0]
                pe_down = 0
                dve_down = [0, 0]
                for tb in range(NCH):
                    sl = tb % 2
                    xb = xb2[sl]
                    xnf = xnf2[sl]
                    t0 = tb * TW
                    if tb + 1 < NCH:
                        x_tok[tb + 1] = load_x_chunk(src, t0 + TW, xb2[1 - sl])
                    K.wait("pe", "dve", ln_tok[tb])
                    for g in range(NG):
                        asl = g % 2
                        K.wait("pe", "dve", t_actg[asl])
                        for j in range(GRP):
                            ib = GRP * g + j
                            wsl = ib % 2
                            K.wait("qg", "pe", pe_w[wsl])
                            w_tok[wsl] = K.dma(
                                "qg", wbuf[wsl].ap().rearrange("p (k f) -> p k f", k=2),
                                ffn8[ib].rearrange("k p f -> p k f"))
                            K.wait("qg", "pe", pe_down)
                            wd_tok = K.dma("qg", wd_grp[:, j * H:(j + 1) * H],
                                           ffn8d[ib][:, :])
                            K.wait("pe", "qg", w_tok[wsl])
                            K.wait("pe", "act", t_silu)
                            for hc in range(NHC):
                                K.emit("pe", lambda e, wsl=wsl, hc=hc, xnf=xnf: e.matmul(
                                    b_g[:, :], wbuf[wsl][:, hc * 128:(hc + 1) * 128],
                                    xnf[:, hc * TW:(hc + 1) * TW],
                                    start=(hc == 0), stop=(hc == NHC - 1)),
                                    inc=(1 if hc == NHC - 1 else 0))
                            t_gmm = K.tok("pe")
                            K.wait("pe", "dve", K.tok("dve"))
                            for hc in range(NHC):
                                K.emit("pe", lambda e, wsl=wsl, hc=hc, xnf=xnf: e.matmul(
                                    b_u[:, :],
                                    wbuf[wsl][:, H + hc * 128:H + (hc + 1) * 128],
                                    xnf[:, hc * TW:(hc + 1) * TW],
                                    start=(hc == 0), stop=(hc == NHC - 1)),
                                    inc=(1 if hc == NHC - 1 else 0))
                            t_umm = K.tok("pe")
                            pe_w[wsl] = t_umm
                            K.wait("act", "pe", t_gmm)
                            K.wait("act", "dve", K.tok("dve"))
                            K.emit("act", lambda e, ib=ib: e.activation(
                                silu_s[:, :], b_g[:, :], AF.Silu,
                                scale=s_gs[:, ib:ib + 1]), inc=1)
                            t_silu = K.tok("act")
                            K.wait("dve", "act", t_silu)
                            K.wait("dve", "pe", t_umm)
                            K.emit("dve", lambda e, ib=ib, asl=asl, j=j:
                                   e.scalar_tensor_tensor(
                                       act_g[asl][:, j * TW:(j + 1) * TW],
                                       b_u[:, :], s_us[:, ib:ib + 1], silu_s[:, :],
                                       ALU.mult, ALU.mult), inc=1)
                        t_actg[asl] = K.tok("dve")
                        K.wait("pe", "qg", wd_tok)
                        K.wait("pe", "dve", t_actg[asl])
                        for oc in range(NHC):
                            dsl = oc % 2
                            K.wait("pe", "dve", dve_down[dsl])
                            for j in range(GRP):
                                K.emit("pe", lambda e, j=j, oc=oc, dsl=dsl, asl=asl:
                                       e.matmul(
                                           b_d[dsl][:, :],
                                           wd_grp[:, j * H + oc * 128:j * H + (oc + 1) * 128],
                                           act_g[asl][:, j * TW:(j + 1) * TW],
                                           start=(j == 0), stop=(j == GRP - 1)),
                                       inc=(1 if j == GRP - 1 else 0))
                            t_dmm = K.tok("pe")
                            K.wait("dve", "pe", t_dmm)
                            K.emit("dve", lambda e, oc=oc, dsl=dsl, xb=xb:
                                   e.scalar_tensor_tensor(
                                       xb[:, oc * TW:(oc + 1) * TW], b_d[dsl][:, :],
                                       s_ds[:, oc:oc + 1], xb[:, oc * TW:(oc + 1) * TW],
                                       ALU.mult, ALU.add), inc=1)
                            dve_down[dsl] = K.tok("dve")
                        pe_down = K.tok("pe")
                        if g == 1 and tb + 1 < NCH:
                            ln_tok[tb + 1] = emit_ln(
                                x_tok[tb + 1], xb2[1 - sl],
                                lambda hc, o=1 - sl: xnf2[o][:, hc * TW:(hc + 1) * TW],
                                scr, bmu, brs, br1, br2)
                    K.wait("qs", "dve", K.tok("dve"))
                    store_x_chunk(dst, t0, xb)
                K.barrier()

        if n_sub >= 2:
            ffn_phase("fb", x_res, x_res)

        def phase_c():
                with ExitStack() as phc:
                    if DEBUG_TAPS:
                        dbg_xres = nc.dram_tensor("dbg_xres", [NHC, 128, S], F32,
                                                  kind="ExternalOutput").ap()
                        K.barrier()
                        K.dma("qs", dbg_xres[:, :, :], x_res[:, :, :])
                        K.wait("qs", "qs", K.tok("qs"))
                    K_rot = alloc(phc, "K_rot", [128, S], BF16)
                    V_tm = alloc(phc, "V_tm", [128, S], BF16)

                    with ExitStack() as c1:
                        xb = alloc(c1, "c1_xb", [128, NHC * TW], F32)
                        xn = alloc(c1, "c1_xn", [128, NHC * S], BF16)
                        scr = ln_scratch(c1, "c1")
                        rot = {nm: alloc(c1, f"c1_{nm}", [128, S], F32)
                               for nm in ("rot_cq", "rot_sq", "rot_ck", "rot_sk")}
                        wb = [alloc(c1, f"c1_w{i}", [128, H], BF16) for i in range(2)]
                        tc_s = alloc(c1, "c1_tc", [128, TW], F32)
                        ts_s = alloc(c1, "c1_ts", [128, TW], F32)
                        qr_s = alloc(c1, "c1_qr", [128, TW], BF16)
                        V_fm = alloc(c1, "c1_vfm", [128, S], BF16)
                        pq = [pbank(c1, f"pC1_{i}") for i in range(2)]
                        b_tr = pbank(c1, "pC1_tr", BF16)
                        bmu = pbank(c1, "pC1_mu")
                        brs = pbank(c1, "pC1_rs")
                        br1 = pbank(c1, "pC1_r1")
                        br2 = pbank(c1, "pC1_r2")

                        for nm, t in rot.items():
                            K.dma("qs", t[:, :], dt_in[nm][:, :])
                        rot_tok = K.tok("qs")
                        ln_prev = 0
                        for tc in range(S // TW):
                            t0 = tc * TW
                            K.wait("qs", "dve", ln_prev)
                            tld = load_x_chunk(x_res, t0, xb)
                            if DEBUG_TAPS and tc == 0:
                                dbg_xb = nc.dram_tensor("dbg_xb", [128, NHC * TW], F32,
                                                        kind="ExternalOutput").ap()
                                K.wait("qs", "qs", tld)
                                K.dma("qs", dbg_xb[:, :], xb[:, :])
                            ln_prev = emit_ln(
                                tld, xb,
                                lambda hc, t0=t0: xn[:, hc * S + t0:hc * S + t0 + TW],
                                scr, bmu, brs, br1, br2)
                        if DEBUG_TAPS:
                            dbg_xn = nc.dram_tensor("dbg_xn", [128, NHC * S], BF16,
                                                    kind="ExternalOutput").ap()
                            K.dma("qs", dbg_xn[:, :], xn[:, :])
                            K.wait("qs", "qs", K.tok("qs"))
                        K.wait("dve", "qs", rot_tok)

                        def rotary_evac(p_src, cosT, sinT, t0, dst_ap, tok_inc):
                            K.emit("dve", lambda e: e.tensor_mul(
                                tc_s[:, :], p_src[:, :], cosT[:, t0:t0 + TW]))
                            K.emit("dve", lambda e: e.tensor_mul(
                                ts_s[0:64, :], p_src[64:128, :], sinT[0:64, t0:t0 + TW]))
                            K.emit("dve", lambda e: e.tensor_mul(
                                ts_s[64:128, :], p_src[0:64, :], sinT[64:128, t0:t0 + TW]))
                            K.emit("dve", lambda e: e.tensor_add(
                                dst_ap, tc_s[:, :], ts_s[:, :]), inc=tok_inc)

                        w_tok = [0, 0]
                        pe_w = [0, 0]
                        ev_p = [0, 0]
                        st_q = 0
                        K.wait("pe", "dve", K.tok("dve"))
                        for j in range(18):
                            is_kv = j >= 16
                            oc = j - 16 if is_kv else j
                            sl = j % 2
                            src = (wkvT if is_kv else wqT)[oc]
                            K.wait("qg", "pe", pe_w[sl])
                            w_tok[sl] = K.dma("qg", wb[sl][:, :], src[:, :])
                            K.wait("pe", "qg", w_tok[sl])
                            for q in range(4):
                                psl = q % 2
                                K.wait("pe", "dve", ev_p[psl])
                                for hc in range(NHC):
                                    K.emit("pe", lambda e, sl=sl, q=q, hc=hc, psl=psl:
                                           e.matmul(
                                               pq[psl][:, :],
                                               wb[sl][:, hc * 128:(hc + 1) * 128],
                                               xn[:, hc * S + q * 512:hc * S + (q + 1) * 512],
                                               start=(hc == 0), stop=(hc == NHC - 1)),
                                           inc=(1 if hc == NHC - 1 else 0))
                                if q == 3:
                                    pe_w[sl] = K.tok("pe")
                                t_mm = K.tok("pe")
                                K.wait("dve", "pe", t_mm)
                                t0 = q * TW
                                if not is_kv:
                                    K.wait("dve", "qs", st_q)
                                    rotary_evac(pq[psl], rot["rot_cq"], rot["rot_sq"], t0,
                                                qr_s[:, :], 1)
                                    ev_p[psl] = K.tok("dve")
                                    K.wait("qs", "dve", ev_p[psl])
                                    st_q = K.dma(
                                        "qs", q_scr[q, :, oc * TW:(oc + 1) * TW],
                                        qr_s[:, :])
                                elif oc == 0:
                                    rotary_evac(pq[psl], rot["rot_ck"], rot["rot_sk"], t0,
                                                K_rot[:, t0:t0 + TW], 1)
                                    ev_p[psl] = K.tok("dve")
                                else:
                                    K.emit("dve", lambda e, t0=t0, psl=psl: e.tensor_copy(
                                        V_fm[:, t0:t0 + TW], pq[psl][:, :]), inc=1)
                                    ev_p[psl] = K.tok("dve")
                        if DEBUG_TAPS:
                            dbg_kr = nc.dram_tensor("dbg_kr", [128, S], BF16,
                                                    kind="ExternalOutput").ap()
                            K.wait("qs", "dve", K.tok("dve"))
                            K.dma("qs", dbg_kr[:, :], K_rot[:, :])
                        K.wait("pe", "dve", K.tok("dve"))
                        for c in range(NHC):
                            K.emit("pe", lambda e, c=c: e.transpose(
                                b_tr[:, (c % 4) * 128:(c % 4 + 1) * 128],
                                V_fm[:, c * 128:(c + 1) * 128], ident[:, :]), inc=1)
                            K.wait("dve", "pe", K.tok("pe"))
                            K.emit("dve", lambda e, c=c: e.tensor_copy(
                                V_tm[:, c * 128:(c + 1) * 128],
                                b_tr[:, (c % 4) * 128:(c % 4 + 1) * 128]), inc=1)
                            K.wait("pe", "dve", K.tok("dve"))
                        if DEBUG_TAPS:
                            dbg_vt = nc.dram_tensor("dbg_vt", [128, S], BF16,
                                                    kind="ExternalOutput").ap()
                            K.wait("qs", "dve", K.tok("dve"))
                            K.dma("qs", dbg_vt[:, :], V_tm[:, :])
                        K.barrier()

                    with ExitStack() as c2:
                        qb = alloc(c2, "c2_q", [128, NH * TW], BF16)
                        x_in = alloc(c2, "c2_xin", [128, NHC * TW], F32)
                        xnew = alloc(c2, "c2_xnew", [128, NHC * TW], F32)
                        attn_o = alloc(c2, "c2_attn", [128, NH * TW], BF16)
                        wpb = alloc(c2, "c2_wp", [128, NHC * H], BF16)
                        rs_row = alloc(c2, "c2_rsr", [1, 2 * TW], F32)
                        rsb_s = alloc(c2, "c2_rsb", [128, TW], F32)
                        accd = [alloc(c2, f"c2_acc{i}", [128, TW], F32) for i in range(2)]
                        pr_s = [alloc(c2, f"c2_pr{i}", [128, TW], BF16) for i in range(3)]

                        b_mm = [pbank(c2, f"pC2_m{i}") for i in range(3)]
                        b_out = [pbank(c2, f"pC2_o{i}") for i in range(2)]
                        b_sum = [pbank(c2, f"pC2_s{i}") for i in range(2)]

                        for fc in range(NHC):
                            K.dma("qg", wpb[:, fc * H:(fc + 1) * H], wprojT[fc][:, :],
                                  ckpt=(fc == NHC - 1))
                        wp_tok = K.tok("qg")

                        mmc = [("dve", 0)] * 3
                        prc = [0, 0, 0]
                        prd = [0, 0, 0]
                        outc = [0, 0]
                        sumc = [0, 0]
                        accc = [0, 0]
                        rr = 0

                        def _flush_pv(p):
                            prb, pi, jt, njt, hp, gate = p
                            K.wait("pe", *gate)
                            if jt == 0:
                                K.wait("pe", "dve", outc[hp])
                            K.emit("pe", lambda e, prb=prb, jt=jt, njt=njt, hp=hp:
                                   e.matmul(
                                       b_out[hp][:, :],
                                       V_tm[:, jt * 128:(jt + 1) * 128], prb[:, :],
                                       start=(jt == 0), stop=(jt == njt - 1)), inc=1)
                            prc[pi] = K.tok("pe")
                        for tb in range(S // TW):
                            t0 = tb * TW
                            q_tok = K.dma("qg", qb[:, :], q_scr[tb][:, :])
                            x_tok = load_x_chunk(x_res, t0, x_in)
                            K.wait("pe", "qg", q_tok)
                            for h in range(NH):
                                hp = h % 2
                                njt = (t0 + TW) // 128
                                pend = None
                                for jt in range(njt):
                                    bi = rr % 3
                                    rr += 1
                                    bk = b_mm[bi]
                                    K.wait("pe", *mmc[bi])
                                    K.emit("pe", lambda e, h=h, jt=jt, bk=bk: e.matmul(
                                        bk[:, :], K_rot[:, jt * 128:(jt + 1) * 128],
                                        qb[:, h * TW:(h + 1) * TW],
                                        start=True, stop=True), inc=1)
                                    t_mm = K.tok("pe")
                                    pi = jt % 3
                                    prb = pr_s[pi]
                                    K.wait("act", "pe", t_mm)
                                    K.wait("act", "pe", prc[pi])
                                    K.wait("act", "dve", prd[pi])
                                    K.emit("act", lambda e, prb=prb, bk=bk: e.activation(
                                        prb[:, :], bk[:, :], AF.Exp), inc=1)
                                    t_exp = K.tok("act")
                                    mmc[bi] = ("act", t_exp)
                                    K.wait("dve", "act", t_exp)
                                    kdiag = jt - 4 * tb
                                    if kdiag >= 0:
                                        K.emit("dve", lambda e, prb=prb, kdiag=kdiag:
                                               e.tensor_mul(
                                                   prb[:, :], prb[:, :],
                                                   s_m01[:, kdiag * TW:(kdiag + 1) * TW]),
                                               inc=1)
                                        pv_gate = ("dve", K.tok("dve"))
                                    else:
                                        pv_gate = ("act", t_exp)
                                    if jt == 0:
                                        K.wait("dve", "pe", accc[hp])
                                        K.emit("dve", lambda e, prb=prb, hp=hp:
                                               e.tensor_copy(accd[hp][:, :], prb[:, :]),
                                               inc=1)
                                    else:
                                        K.emit("dve", lambda e, prb=prb, hp=hp:
                                               e.tensor_add(accd[hp][:, :],
                                                            accd[hp][:, :], prb[:, :]),
                                               inc=1)
                                    prd[pi] = K.tok("dve")
                                    if pend is not None:
                                        _flush_pv(pend)
                                    pend = (prb, pi, jt, njt, hp, pv_gate)
                                _flush_pv(pend)
                                t_acc = K.tok("dve")
                                K.wait("pe", "dve", t_acc)
                                K.wait("pe", "dve", sumc[hp])
                                K.emit("pe", lambda e, hp=hp: e.matmul(
                                    b_sum[hp][0:1, :], ones_col_f32[:, :],
                                    accd[hp][:, :], start=True, stop=True), inc=1)
                                accc[hp] = K.tok("pe")
                                K.wait("dve", "pe", accc[hp])
                                K.emit("dve", lambda e, hp=hp: e.reciprocal(
                                    rs_row[0:1, hp * TW:(hp + 1) * TW],
                                    b_sum[hp][0:1, :]), inc=1)
                                sumc[hp] = K.tok("dve")
                                bi = rr % 3
                                rr += 1
                                bk = b_mm[bi]
                                K.wait("pe", *mmc[bi])
                                K.wait("pe", "dve", sumc[hp])
                                K.emit("pe", lambda e, hp=hp, bk=bk: e.matmul(
                                    bk[:, :], ones_row_f32[:, :],
                                    rs_row[0:1, hp * TW:(hp + 1) * TW],
                                    start=True, stop=True), inc=1)
                                K.wait("dve", "pe", K.tok("pe"))
                                K.emit("dve", lambda e, bk=bk: e.tensor_copy(
                                    rsb_s[:, :], bk[:, :]))
                                K.emit("dve", lambda e, h=h, hp=hp: e.tensor_mul(
                                    attn_o[:, h * TW:(h + 1) * TW], b_out[hp][:, :],
                                    rsb_s[:, :]), inc=1)
                                outc[hp] = K.tok("dve")
                                mmc[bi] = ("dve", outc[hp])
                            t_attn = K.tok("dve")
                            K.wait("pe", "dve", t_attn)
                            K.wait("pe", "qg", wp_tok)
                            K.wait("dve", "qs", x_tok)
                            for fc in range(NHC):
                                bi = rr % 3
                                rr += 1
                                bk = b_mm[bi]
                                K.wait("pe", *mmc[bi])
                                for oc in range(NHC):
                                    K.emit("pe", lambda e, fc=fc, oc=oc, bk=bk: e.matmul(
                                        bk[:, :],
                                        wpb[:, fc * H + oc * 128:fc * H + (oc + 1) * 128],
                                        attn_o[:, oc * TW:(oc + 1) * TW],
                                        start=(oc == 0), stop=(oc == NHC - 1)),
                                        inc=(1 if oc == NHC - 1 else 0))
                                t_y = K.tok("pe")
                                K.wait("dve", "pe", t_y)
                                K.emit("dve", lambda e, fc=fc, bk=bk:
                                       e.scalar_tensor_tensor(
                                           xnew[:, fc * TW:(fc + 1) * TW], bk[:, :],
                                           s_bproj[:, fc:fc + 1],
                                           x_in[:, fc * TW:(fc + 1) * TW],
                                           ALU.add, ALU.add), inc=1)
                                mmc[bi] = ("dve", K.tok("dve"))
                            K.wait("qs", "dve", K.tok("dve"))
                            store_x_chunk(x_res, t0, xnew)
                            K.barrier()
                            mmc = [("dve", 0)] * 3
                            prc = [0, 0, 0]
                            prd = [0, 0, 0]
                            outc = [0, 0]
                            sumc = [0, 0]
                            accc = [0, 0]

        if n_sub >= 3:
            phase_c()


        if n_sub >= 4:
            ffn_phase("fd", x_res, yT)
            fin = K.tok("qs")
        else:
            K.barrier()
            fin = K.dma("qs", yT[:, :, :], x_res[:, :, :])
        if DEBUG_TAPS:
            dbg_qkv = nc.dram_tensor("dbg_qkv", [NBLK, 48, 128, BLK], BF16,
                                     kind="ExternalOutput").ap()
            dbg_gate = nc.dram_tensor("dbg_gate", [NBLK, 16, 128, BLK], BF16,
                                      kind="ExternalOutput").ap()
            K.barrier()
            K.dma("qs", dbg_qkv[:, :, :, :], qkv_scr[:, :, :, :])
            fin = K.dma("qs", dbg_gate[:, :, :, :], gate_scr[:, :, :, :])
            if n_sub >= 3:
                dbg_q = nc.dram_tensor("dbg_q", [S // TW, 16, 128, TW], BF16,
                                       kind="ExternalOutput").ap()
                fin = K.dma("qs", dbg_q[:, :, :, :], q_scr[:, :, :, :])
        K.wait("qs", "qs", fin)
        K.flush()

    return nc



def _prep_shared(inputs):
    sh = {}
    sh["wqkvT"] = _tile_lhsT_oc(np.asarray(inputs["L0_attn_a_w_qkv"], dtype=np.float32))
    sh["wgateT"] = _tile_lhsT_oc(np.asarray(inputs["L0_attn_a_w_gate"], dtype=np.float32))
    sh["woutT"] = _tile_lhsT_oc(np.asarray(inputs["L0_attn_a_w_out"], dtype=np.float32))
    sh["wqT"] = _tile_lhsT_oc(np.asarray(inputs["L0_attn_b_w_q"], dtype=np.float32))
    sh["wkvT"] = _tile_lhsT_oc(np.asarray(inputs["L0_attn_b_w_kv"], dtype=np.float32))
    sh["wprojT"] = _tile_lhsT_oc(np.asarray(inputs["L0_attn_b_w_proj"], dtype=np.float32))
    sh["slope"] = np.ascontiguousarray(
        np.asarray(inputs["L0_attn_a_slope_rate"], dtype=np.float32).reshape(1, NH))
    sh["normw"] = np.ascontiguousarray(
        np.asarray(inputs["L0_attn_a_norm_w"], dtype=np.float32).reshape(NHC, 128).T)
    sh["bproj"] = np.ascontiguousarray(
        np.asarray(inputs["L0_attn_b_b_proj"], dtype=np.float32).reshape(NHC, 128).T)

    def enc(w):
        u8 = np.asarray(w).astype(np.uint8)
        return (u8 ^ np.uint8(0x80)).view(np.int8)

    g8 = _tile_lhsT_oc(enc(inputs["L0_ffn_gate_w"]))
    u8_ = _tile_lhsT_oc(enc(inputs["L0_ffn_up_w"]))
    d8 = _tile_lhsT_hc(enc(inputs["L0_ffn_down_w"]))
    sh["ffn8"] = np.ascontiguousarray(np.stack([g8, u8_], axis=1))
    sh["ffn8d"] = np.ascontiguousarray(d8)
    sh["gs_t"] = np.ascontiguousarray(
        np.asarray(inputs["L0_ffn_gate_scale"], dtype=np.float32).reshape(NIC, 128).T)
    sh["us_t"] = np.ascontiguousarray(
        np.asarray(inputs["L0_ffn_up_scale"], dtype=np.float32).reshape(NIC, 128).T)
    sh["ds_t"] = np.ascontiguousarray(
        np.asarray(inputs["L0_ffn_down_scale"], dtype=np.float32).reshape(NHC, 128).T)
    for nm, arr in _host_constants().items():
        sh[nm] = arr
    for nm, arr in _rotary_tables().items():
        sh[nm] = arr
    return sh


def kernel(**inputs):
    x = np.asarray(inputs["L0_attn_a_hidden_states"], dtype=np.float32)
    sh = _prep_shared(inputs)
    nc = build_program(N_SUB)
    in_maps = []
    for c in range(B):
        m = dict(sh)
        m["xT"] = np.ascontiguousarray(x[c].T).reshape(NHC, 128, S)
        in_maps.append(m)
    res = run_bass_kernel_spmd(nc, in_maps, core_ids=list(range(B)))
    globals()["_last_res"] = res
    out = np.empty((B, S, H), dtype=np.float32)
    for c in range(B):
        out[c] = res.results[c]["yT"].reshape(H, S).T
    return out


if __name__ == "__main__":
    import reference as _r
    inp = _r.setup_inputs()
    out = kernel(**{k: np.asarray(v) for k, v in inp.items()})
    print("out", out.shape, out.dtype, "finite:", np.isfinite(out).all())


# version 14 (final)
# speedup vs baseline: 1.0139x; 1.0139x over previous
import numpy as np
import ml_dtypes
from contextlib import ExitStack

import concourse.bass as bass
import concourse.mybir as mybir
from concourse.bass_utils import run_bass_kernel_spmd

F32 = mybir.dt.float32
BF16 = mybir.dt.bfloat16
I8 = mybir.dt.int8
AF = mybir.ActivationFunctionType
ALU = mybir.AluOpType
AX = mybir.AxisListType

B, S, H = 8, 2048, 2048
NH, HD = 16, 128
BLK = 256
NBLK = S // BLK
DI = 8192
NHC = H // 128
NIC = DI // 128
TW = 512

N_SUB = 4
DEBUG_TAPS = False

NEG_BIG = -1.0e30


def _tile_lhsT_oc(Wnp):
    O, Hd = Wnp.shape
    a = Wnp.reshape(O // 128, 128, Hd // 128, 128)
    a = a.transpose(0, 3, 2, 1)
    return np.ascontiguousarray(a.reshape(O // 128, 128, Hd))


def _tile_lhsT_hc(Wnp):
    O, Hd = Wnp.shape
    a = Wnp.reshape(O // 128, 128, Hd // 128, 128)
    a = a.transpose(2, 3, 0, 1)
    return np.ascontiguousarray(a.reshape(Hd // 128, 128, O))


def _host_constants():
    c = {}
    c["ident_bf"] = np.eye(128, dtype=ml_dtypes.bfloat16)
    c["ident_f32"] = np.eye(128, dtype=np.float32)
    c["ones_row_f32"] = np.ones((1, 128), dtype=np.float32)
    c["ones_row_bf"] = np.ones((1, 128), dtype=ml_dtypes.bfloat16)
    c["ones_col_f32"] = np.ones((128, 1), dtype=np.float32)
    c["ones_col_bf"] = np.ones((128, 1), dtype=ml_dtypes.bfloat16)
    m = np.arange(BLK, dtype=np.float32)
    diffT = np.empty((2, 128, BLK), dtype=np.float32)
    for nh in range(2):
        n_abs = nh * 128 + np.arange(128, dtype=np.float32)
        d = m[None, :] - n_abs[:, None]
        d[d < 0] = 1.0e9
        diffT[nh] = d
    c["diffT"] = diffT
    c["qd_row"] = np.arange(1, BLK + 1, dtype=np.float32).reshape(1, BLK)
    kd = np.empty((128, 2), dtype=np.float32)
    for nh in range(2):
        kd[:, nh] = 255.0 - (nh * 128 + np.arange(128, dtype=np.float32))
    c["kd_col"] = kd
    ii = np.arange(128)
    tri = np.where(ii[None, :] <= ii[:, None], 0.0, NEG_BIG).astype(np.float32)
    c["tri_diag"] = tri
    i512 = np.arange(TW)
    m01 = np.zeros((4, 128, TW), dtype=np.float32)
    for k in range(4):
        m01[k] = (ii[:, None] + 128 * k <= i512[None, :]).astype(np.float32)
    c["m01"] = m01.astype(ml_dtypes.bfloat16)
    return c


def _rotary_tables():
    dim = HD
    sw = dim ** (-0.5)
    i = np.arange(0, dim, 2, dtype=np.float32)
    inv_freq = 1.0 / (10000.0 ** (i / dim))
    t = np.arange(S, dtype=np.float32)
    freqs = np.concatenate([np.outer(t, inv_freq)] * 2, axis=-1)
    scale_base = (i + 0.4 * dim) / (1.4 * dim)
    power = (t - S // 2) / 512.0
    xs = scale_base[None, :] ** power[:, None]
    xs = np.concatenate([xs, xs], axis=-1).astype(np.float32)
    cos = np.cos(freqs).astype(np.float32)
    sin = np.sin(freqs).astype(np.float32)

    def fold(scale):
        C = (sw * cos * scale).astype(np.float32)
        Sr = (sw * sin * scale).astype(np.float32)
        Sf = np.empty_like(Sr)
        Sf[:, :64] = -Sr[:, :64]
        Sf[:, 64:] = Sr[:, 64:]
        return np.ascontiguousarray(C.T), np.ascontiguousarray(Sf.T)

    cq, sq = fold(xs)
    ck, sk = fold((1.0 / xs).astype(np.float32))
    return {"rot_cq": cq, "rot_sq": sq, "rot_ck": ck, "rot_sk": sk}



class _Eng:
    def __init__(self, attr):
        self.attr = attr
        self.ops = []
        self.count = 0
        self.sem = None
        self.dsem = None
        self.issued = 0
        self.pending = 0
        self.memo = {}
        self.pending_barrier = None


class KB:
    ENG = {"pe": "tensor", "dve": "vector", "act": "scalar",
           "qs": "sync", "qg": "gpsimd"}

    def __init__(self, nc, block, ctx):
        self.nc = nc
        self.block = block
        self.e = {n: _Eng(a) for n, a in self.ENG.items()}
        for n, en in self.e.items():
            en.sem = ctx.enter_context(nc.semaphore(f"sem_{n}"))
            if n in ("qs", "qg"):
                en.dsem = ctx.enter_context(nc.semaphore(f"dsem_{n}"))

    def _flush_barrier(self, name):
        en = self.e[name]
        if en.pending_barrier is not None:
            snap = en.pending_barrier
            en.pending_barrier = None
            for other, val in snap.items():
                if other != name and val > 0:
                    self.wait(name, other, val)

    def emit(self, name, fn, inc=0):
        en = self.e[name]
        self._flush_barrier(name)
        tok = None
        if inc:
            en.count += inc
            tok = en.count
            sem = en.sem

            def wrapped(e, fn=fn, sem=sem, inc=inc):
                fn(e).then_inc(sem, inc)
            en.ops.append(wrapped)
        else:
            en.ops.append(fn)
        return tok

    def wait(self, name, producer, val):
        if val is None or val <= 0:
            return
        en = self.e[name]
        self._flush_barrier(name)
        assert val <= self.e[producer].count, (
            f"wait on future token {producer}={val} > {self.e[producer].count}")
        if en.memo.get(producer, 0) >= val:
            return
        en.memo[producer] = val
        sem = self.e[producer].sem
        en.ops.append(lambda e, sem=sem, val=val: e.wait_ge(sem, val))

    def dma(self, qname, out, in_, ckpt=True, **kw):
        en = self.e[qname]
        self._flush_barrier(qname)
        en.issued += 1
        en.pending += 1
        dsem = en.dsem

        def op(e, out=out, in_=in_, kw=kw, dsem=dsem):
            e.dma_start(out=out, in_=in_, **kw).then_inc(dsem, 16)
        en.ops.append(op)
        if ckpt:
            return self.ckpt(qname)
        return None

    def ckpt(self, qname):
        en = self.e[qname]
        if en.pending == 0:
            return en.count
        en.pending = 0
        en.count += 1
        tok = en.count
        target = 16 * en.issued
        dsem, csem = en.dsem, en.sem
        en.ops.append(lambda e, dsem=dsem, target=target, csem=csem:
                      e.wait_ge(dsem, target).then_inc(csem, 1))
        return tok

    def barrier(self):
        snap = {n: en.count for n, en in self.e.items()}
        for en in self.e.values():
            en.pending_barrier = dict(snap)

    def flush(self):
        for n, en in self.e.items():
            if not en.ops:
                continue
            ops = en.ops
            en.ops = []
            getattr(self.block, en.attr)(lambda eng, ops=ops: [f(eng) for f in ops])

    def tok(self, name):
        return self.e[name].count



def build_program(n_sub=N_SUB):
    nc = bass.Bass("TRN2", target_bir_lowering=False, debug=False)
    dt_in = {}

    def din(name, shape, dtype):
        dt_in[name] = nc.dram_tensor(name, list(shape), dtype, kind="ExternalInput").ap()
        return dt_in[name]

    xT = din("xT", [NHC, 128, S], F32)
    wqkvT = din("wqkvT", [48, 128, H], F32)
    wgateT = din("wgateT", [16, 128, H], F32)
    woutT = din("woutT", [16, 128, H], F32)
    wqT = din("wqT", [16, 128, H], F32)
    wkvT = din("wkvT", [2, 128, H], F32)
    wprojT = din("wprojT", [16, 128, H], F32)
    din("slope", [1, NH], F32)
    din("normw", [128, NHC], F32)
    din("bproj", [128, NHC], F32)
    ffn8 = din("ffn8", [NIC, 2, 128, H], I8)
    ffn8d = din("ffn8d", [NIC, 128, H], I8)
    din("gs_t", [128, NIC], F32)
    din("us_t", [128, NIC], F32)
    din("ds_t", [128, NHC], F32)
    consts = _host_constants()
    for nm, arr in consts.items():
        din(nm, arr.shape, F32 if arr.dtype == np.dtype(np.float32) else BF16)
    for nm in ("rot_cq", "rot_sq", "rot_ck", "rot_sk"):
        din(nm, [128, S], F32)

    yT = nc.dram_tensor("yT", [NHC, 128, S], F32, kind="ExternalOutput").ap()
    x_res = nc.dram_tensor("x_res", [NHC, 128, S], F32, kind="Internal").ap()
    qkv_scr = nc.dram_tensor("qkv_scr", [NBLK, 128, 48 * BLK], BF16, kind="Internal").ap()
    gate_scr = nc.dram_tensor("gate_scr", [NBLK, 128, 16 * BLK], BF16, kind="Internal").ap()
    gated_scr = nc.dram_tensor("gated_scr", [NBLK, 128, 16 * BLK], BF16,
                               kind="Internal").ap()
    q_scr = nc.dram_tensor("q_scr", [S // TW, 128, 16 * TW], BF16, kind="Internal").ap()

    with ExitStack() as top:
        block = top.enter_context(nc.Block())
        K = KB(nc, block, top)

        def alloc(ctx, name, shape, dtype):
            return ctx.enter_context(nc.sbuf_tensor(name, list(shape), dtype))

        def pbank(ctx, name, dtype=F32):
            return ctx.enter_context(nc.psum_tensor(
                name, [128, 512 if dtype == F32 else 1024], dtype))

        ident = alloc(top, "sb_ident", [128, 128], BF16)
        ident32 = alloc(top, "sb_ident32", [128, 128], F32)
        ones_row_f32 = alloc(top, "sb_ones_row_f32", [1, 128], F32)
        ones_row_bf = alloc(top, "sb_ones_row_bf", [1, 128], BF16)
        ones_col_f32 = alloc(top, "sb_ones_col_f32", [128, 1], F32)
        ones_col_bf = alloc(top, "sb_ones_col_bf", [128, 1], BF16)
        s_normw = alloc(top, "s_normw", [128, NHC], F32)
        s_bproj = alloc(top, "s_bproj", [128, NHC], F32)
        s_gs = alloc(top, "s_gs", [128, NIC], F32)
        s_us = alloc(top, "s_us", [128, NIC], F32)
        s_ds = alloc(top, "s_ds", [128, NHC], F32)
        s_tri = alloc(top, "s_tri", [128, 128], F32)
        s_m01 = alloc(top, "s_m01", [128, 4 * TW], BF16)

        for nm, t in [("ident_bf", ident), ("ident_f32", ident32),
                      ("ones_row_f32", ones_row_f32), ("ones_row_bf", ones_row_bf),
                      ("ones_col_f32", ones_col_f32), ("ones_col_bf", ones_col_bf),
                      ("normw", s_normw), ("bproj", s_bproj), ("gs_t", s_gs),
                      ("us_t", s_us), ("ds_t", s_ds), ("tri_diag", s_tri)]:
            K.dma("qs", t[:, :], dt_in[nm][:, :])
        K.dma("qs", s_m01.ap().rearrange("p (k f) -> p k f", k=4),
              dt_in["m01"].rearrange("k p f -> p k f"))
        tok_consts = K.tok("qs")

        def emit_ln(x_tok, xb, xn_dst, scr, bk_mu, bk_rs, bk_r1, bk_r2, eps=1e-5):
            tw = TW
            K.wait("act", "qs", x_tok)
            K.wait("dve", "qs", x_tok)
            K.wait("dve", "pe", scr.get("_t_rows", 0))
            xv = xb.ap().rearrange("p (hc t) -> p t hc", hc=NHC)
            K.emit("dve", lambda e, xv=xv: e.tensor_reduce(
                scr["acc"][:, :], xv, AX.X, ALU.add), inc=1)
            for hc in range(NHC):
                sq = scr["sq"][:, (hc % 2) * tw:(hc % 2 + 1) * tw]
                if hc >= 2:
                    K.wait("act", "dve", scr["_sqtok"][hc - 2])
                elif (NHC - 2 + hc) in scr["_sqtok"]:
                    K.wait("act", "dve", scr["_sqtok"][NHC - 2 + hc])
                K.emit("act", lambda e, hc=hc, sq=sq: e.activation(
                    sq, xb[:, hc * tw:(hc + 1) * tw], AF.Square), inc=1)
                K.wait("dve", "act", K.tok("act"))
                if hc == 0:
                    K.emit("dve", lambda e, sq=sq: e.tensor_copy(scr["acc2"][:, :], sq),
                           inc=1)
                else:
                    K.emit("dve", lambda e, sq=sq: e.tensor_add(
                        scr["acc2"][:, :], scr["acc2"][:, :], sq), inc=1)
                scr["_sqtok"][hc] = K.tok("dve")
            t_acc = K.tok("dve")
            K.wait("pe", "dve", t_acc)
            K.emit("pe", lambda e: e.matmul(bk_r1[0:1, :], ones_col_f32[:, :],
                                            scr["acc"][:, :], start=True, stop=True))
            K.emit("pe", lambda e: e.matmul(bk_r2[0:1, :], ones_col_f32[:, :],
                                            scr["acc2"][:, :], start=True, stop=True),
                   inc=1)
            t_rows = K.tok("pe")
            scr["_t_rows"] = t_rows
            K.wait("dve", "pe", t_rows)
            K.wait("dve", "act", scr.get("_t_rstd", 0))
            r_nm, r_ss, r_t = scr["row_a"], scr["row_b"], scr["row_c"]
            K.emit("dve", lambda e: e.tensor_scalar_mul(r_nm[0:1, :], bk_r1[0:1, :],
                                                        -1.0 / H))
            K.emit("dve", lambda e: e.tensor_scalar_mul(r_ss[0:1, :], bk_r2[0:1, :],
                                                        1.0 / H))
            K.emit("dve", lambda e: e.tensor_mul(r_t[0:1, :], r_nm[0:1, :], r_nm[0:1, :]))
            K.emit("dve", lambda e, eps=eps: e.scalar_tensor_tensor(
                r_ss[0:1, :], r_ss[0:1, :], float(eps), r_t[0:1, :],
                ALU.add, ALU.subtract), inc=1)
            t_var = K.tok("dve")
            K.wait("act", "dve", t_var)
            K.emit("act", lambda e: e.activation(r_ss[0:1, :], r_ss[0:1, :], AF.Ln))
            K.emit("act", lambda e: e.activation(r_ss[0:1, :], r_ss[0:1, :],
                                                 AF.Exp, scale=-0.5), inc=1)
            t_rstd = K.tok("act")
            scr["_t_rstd"] = t_rstd
            K.wait("pe", "dve", t_var)
            K.wait("pe", "act", t_rstd)
            K.emit("pe", lambda e: e.matmul(bk_mu[:, :], ones_row_f32[:, :],
                                            r_nm[0:1, :], start=True, stop=True))
            K.emit("pe", lambda e: e.matmul(bk_rs[:, :], ones_row_f32[:, :],
                                            r_ss[0:1, :], start=True, stop=True), inc=1)
            t_bc = K.tok("pe")
            K.wait("dve", "pe", t_bc)
            K.emit("dve", lambda e: e.tensor_copy(scr["mu_b"][:, :], bk_mu[:, :]))
            K.emit("dve", lambda e: e.tensor_copy(scr["rs_b"][:, :], bk_rs[:, :]), inc=1)
            for hc in range(NHC):
                K.emit("dve", lambda e, hc=hc: e.tensor_add(
                    scr["t1"][:, :], xb[:, hc * tw:(hc + 1) * tw], scr["mu_b"][:, :]))
                K.emit("dve", lambda e, hc=hc: e.tensor_mul(
                    xn_dst(hc), scr["t1"][:, :], scr["rs_b"][:, :]),
                    inc=(1 if hc == NHC - 1 else 0))
            return K.tok("dve")

        def ln_scratch(ctx, tag):
            return {
                "sq": alloc(ctx, f"{tag}_sq", [128, 2 * TW], F32),
                "acc": alloc(ctx, f"{tag}_acc", [128, TW], F32),
                "acc2": alloc(ctx, f"{tag}_acc2", [128, TW], F32),
                "t1": alloc(ctx, f"{tag}_t1", [128, TW], F32),
                "mu_b": alloc(ctx, f"{tag}_mub", [128, TW], F32),
                "rs_b": alloc(ctx, f"{tag}_rsb", [128, TW], F32),
                "row_a": alloc(ctx, f"{tag}_ra", [1, TW], F32),
                "row_b": alloc(ctx, f"{tag}_rb", [1, TW], F32),
                "row_c": alloc(ctx, f"{tag}_rc", [1, TW], F32),
                "_sqtok": {},
            }

        def load_x_chunk(src, t0, xb, tw=TW):
            return K.dma("qs", xb.ap().rearrange("p (hc t) -> p hc t", hc=NHC),
                         src[:, :, t0:t0 + tw].rearrange("hc p t -> p hc t"))

        def store_x_chunk(dst, t0, xb, tw=TW):
            return K.dma("qs", dst[:, :, t0:t0 + tw].rearrange("hc p t -> p hc t"),
                         xb.ap().rearrange("p (hc t) -> p hc t", hc=NHC))

        def phase_a():
            with ExitStack() as pha:
                s_diffT = alloc(pha, "s_diffT", [128, 2 * BLK], F32)
                s_qdrow = alloc(pha, "s_qdrow", [1, BLK], F32)
                s_kdcol = alloc(pha, "s_kdcol", [128, 2], F32)
                s_srow = alloc(pha, "s_srow", [1, NH], F32)
                s_negs = alloc(pha, "s_negs", [128, NH], F32)
                s_bd = alloc(pha, "s_bd", [128, NH], F32)
                maskT = alloc(pha, "maskT", [128, NH * 2 * BLK], BF16)
                qdec = alloc(pha, "qdec", [128, NH * BLK], BF16)
                kdec = alloc(pha, "kdec", [128, 2 * NH], F32)
                kv32 = alloc(pha, "kv32", [128, NH * HD], F32)
                kvbf = alloc(pha, "kvbf", [128, NH * HD], BF16)
                tmpA = alloc(pha, "tmpA", [128, 4 * BLK], F32)
                tmpB = alloc(pha, "tmpB", [128, 4], F32)
                rowA = alloc(pha, "rowA", [1, 2 * BLK], F32)
                rowB = alloc(pha, "rowB", [1, NH], F32)

                with ExitStack() as ph0:
                    p_set = pbank(ph0, "pA_set")
                    p_set2 = pbank(ph0, "pA_set2")
                    K.dma("qs", s_diffT.ap().rearrange("p (n f) -> p n f", n=2),
                          dt_in["diffT"].rearrange("n p f -> p n f"))
                    K.dma("qs", s_qdrow[:, :], dt_in["qd_row"][:, :])
                    K.dma("qs", s_kdcol[:, :], dt_in["kd_col"][:, :])
                    t_sl = K.dma("qs", s_srow[:, :], dt_in["slope"][:, :])
                    K.wait("dve", "qs", t_sl)
                    K.wait("pe", "qs", tok_consts)
                    K.emit("dve", lambda e: e.tensor_scalar_mul(rowB[0:1, :], s_srow[0:1, :],
                                                                -1.0), inc=1)
                    K.wait("pe", "dve", K.tok("dve"))
                    K.emit("pe", lambda e: e.matmul(p_set[:, 0:NH], ones_row_f32[:, :],
                                                    rowB[0:1, :], start=True, stop=True),
                           inc=1)
                    K.wait("dve", "pe", K.tok("pe"))
                    K.emit("dve", lambda e: e.tensor_copy(s_negs[:, :], p_set[:, 0:NH]),
                           inc=1)
                    K.emit("dve", lambda e: e.tensor_scalar_mul(rowB[0:1, :], s_srow[0:1, :],
                                                                -float(BLK)), inc=1)
                    K.wait("act", "dve", K.tok("dve"))
                    K.emit("act", lambda e: e.activation(rowB[0:1, :], rowB[0:1, :], AF.Exp),
                           inc=1)
                    K.wait("pe", "act", K.tok("act"))
                    K.wait("pe", "dve", K.tok("dve"))
                    K.emit("pe", lambda e: e.matmul(p_set[:, 0:NH], ones_row_f32[:, :],
                                                    rowB[0:1, :], start=True, stop=True),
                           inc=1)
                    K.wait("dve", "pe", K.tok("pe"))
                    K.emit("dve", lambda e: e.tensor_copy(s_bd[:, :], p_set[:, 0:NH]), inc=1)
                    K.barrier()
                    tAa = [0, 0]
                    tBa = [0, 0]
                    rAa = [0, 0]
                    rApe = [0, 0]
                    psd = [0, 0]
                    pbk = [p_set, p_set2]
                    for h in range(NH):
                        hp = h % 2
                        off = hp * 2 * BLK
                        K.wait("dve", "act", tAa[hp])
                        for nh in range(2):
                            K.emit("dve", lambda e, h=h, nh=nh, off=off: e.tensor_scalar(
                                tmpA[:, off + nh * BLK:off + (nh + 1) * BLK],
                                s_diffT[:, nh * BLK:(nh + 1) * BLK],
                                s_negs[:, h:h + 1], None, ALU.mult), inc=(nh == 1))
                        K.wait("act", "dve", K.tok("dve"))
                        for nh in range(2):
                            K.emit("act", lambda e, h=h, nh=nh, off=off: e.activation(
                                maskT[:, (2 * h + nh) * BLK:(2 * h + nh + 1) * BLK],
                                tmpA[:, off + nh * BLK:off + (nh + 1) * BLK], AF.Exp),
                                inc=(nh == 1))
                        tAa[hp] = K.tok("act")
                        K.wait("dve", "act", tBa[hp])
                        K.emit("dve", lambda e, h=h, hp=hp: e.tensor_scalar(
                            tmpB[:, hp * 2:hp * 2 + 2], s_kdcol[:, :],
                            s_negs[:, h:h + 1], None, ALU.mult), inc=1)
                        K.wait("act", "dve", K.tok("dve"))
                        K.emit("act", lambda e, h=h, hp=hp: e.activation(
                            kdec[:, 2 * h:2 * h + 2], tmpB[:, hp * 2:hp * 2 + 2],
                            AF.Exp), inc=1)
                        tBa[hp] = K.tok("act")
                        K.wait("dve", "act", rAa[hp])
                        K.wait("dve", "pe", rApe[hp])
                        K.emit("dve", lambda e, h=h, hp=hp: e.tensor_scalar(
                            rowA[0:1, hp * BLK:(hp + 1) * BLK], s_qdrow[0:1, :],
                            s_negs[0:1, h:h + 1], None, ALU.mult), inc=1)
                        K.wait("act", "dve", K.tok("dve"))
                        K.emit("act", lambda e, hp=hp: e.activation(
                            rowA[0:1, hp * BLK:(hp + 1) * BLK],
                            rowA[0:1, hp * BLK:(hp + 1) * BLK], AF.Exp), inc=1)
                        rAa[hp] = K.tok("act")
                        K.wait("pe", "act", rAa[hp])
                        K.wait("pe", "dve", psd[hp])
                        K.emit("pe", lambda e, hp=hp: e.matmul(
                            pbk[hp][:, 0:BLK], ones_row_f32[:, :],
                            rowA[0:1, hp * BLK:(hp + 1) * BLK],
                            start=True, stop=True), inc=1)
                        rApe[hp] = K.tok("pe")
                        K.wait("dve", "pe", rApe[hp])
                        K.emit("dve", lambda e, h=h, hp=hp: e.tensor_copy(
                            qdec[:, h * BLK:(h + 1) * BLK], pbk[hp][:, 0:BLK]), inc=1)
                        psd[hp] = K.tok("dve")
                    K.emit("dve", lambda e: e.memset(kv32[:, :], 0.0))
                    K.emit("dve", lambda e: e.memset(kvbf[:, :], 0.0), inc=1)

                with ExitStack() as a1:
                    xb = alloc(a1, "a1_xb", [128, NHC * TW], F32)
                    xn = alloc(a1, "a1_xn", [128, NHC * S], BF16)
                    scr = ln_scratch(a1, "a1")
                    wbuf = [alloc(a1, f"a1_w{i}", [128, H], BF16) for i in range(2)]
                    qgbuf = [alloc(a1, f"a1_qg{i}", [128, S], BF16) for i in range(2)]
                    pq = [pbank(a1, f"pA1_{i}") for i in range(2)]
                    bmu = pbank(a1, "pA1_mu")
                    brs = pbank(a1, "pA1_rs")
                    br1 = pbank(a1, "pA1_r1")
                    br2 = pbank(a1, "pA1_r2")

                    ln_prev = 0
                    for tc in range(S // TW):
                        t0 = tc * TW
                        K.wait("qs", "dve", ln_prev)
                        tld = load_x_chunk(xT, t0, xb)
                        ln_prev = emit_ln(
                            tld, xb,
                            lambda hc, t0=t0: xn[:, hc * S + t0:hc * S + t0 + TW],
                            scr, bmu, brs, br1, br2)
                    xn_tok = K.tok("dve")

                    w_tok = [0, 0]
                    pe_w = [0, 0]
                    ev_p = [0, 0]
                    st_tok = [0, 0]
                    K.wait("pe", "dve", xn_tok)
                    for j in range(64):
                        is_gate = j >= 48
                        oc = j - 48 if is_gate else j
                        sl = j % 2
                        src = (wgateT if is_gate else wqkvT)[oc]
                        K.wait("qg", "pe", pe_w[sl])
                        w_tok[sl] = K.dma("qg", wbuf[sl][:, :], src[:, :])
                        K.wait("pe", "qg", w_tok[sl])
                        for q in range(4):
                            psl = q % 2
                            K.wait("pe", "act", ev_p[psl])
                            for hc in range(NHC):
                                K.emit("pe", lambda e, sl=sl, q=q, hc=hc, psl=psl: e.matmul(
                                    pq[psl][:, :], wbuf[sl][:, hc * 128:(hc + 1) * 128],
                                    xn[:, hc * S + q * 512:hc * S + (q + 1) * 512],
                                    start=(hc == 0), stop=(hc == NHC - 1)),
                                    inc=(1 if hc == NHC - 1 else 0))
                            if q == 3:
                                pe_w[sl] = K.tok("pe")
                            K.wait("act", "pe", K.tok("pe"))
                            K.wait("act", "qs", st_tok[sl])
                            fn = AF.Sigmoid if is_gate else AF.Silu
                            K.emit("act", lambda e, sl=sl, q=q, fn=fn, psl=psl: e.activation(
                                qgbuf[sl][:, q * 512:(q + 1) * 512], pq[psl][:, :], fn),
                                inc=1)
                            ev_p[psl] = K.tok("act")
                        K.wait("qs", "act", K.tok("act"))
                        dst = (gate_scr if is_gate else qkv_scr)
                        st_tok[sl] = K.dma(
                            "qs",
                            dst[:, :, oc * BLK:(oc + 1) * BLK].rearrange(
                                "tb p f -> p tb f"),
                            qgbuf[sl].ap().rearrange("p (tb f) -> p tb f", tb=NBLK))
                    K.barrier()

                with ExitStack() as a2:
                    TB = BLK
                    qkvb = [alloc(a2, f"a2_qkv{i}", [128, 48 * TB], BF16) for i in range(2)]
                    gateb = [alloc(a2, f"a2_g{i}", [128, 16 * TB], BF16) for i in range(2)]
                    attnb = alloc(a2, "a2_attn", [128, NHC * TB], F32)
                    gatedb = alloc(a2, "a2_gated", [128, NHC * TB], BF16)
                    kT_s = [alloc(a2, f"a2_kT{i}", [128, 2 * 128], BF16) for i in range(2)]
                    vT_s = [alloc(a2, f"a2_vT{i}", [128, 2 * 128], BF16) for i in range(2)]
                    qs_s = [alloc(a2, f"a2_qs{i}", [128, TB], BF16) for i in range(2)]
                    pT_s = [alloc(a2, f"a2_pT{i}", [128, 2 * TB], BF16) for i in range(2)]
                    sq_s = alloc(a2, "a2_sq", [128, 2 * TB], F32)
                    rms_acc = alloc(a2, "a2_racc", [128, TB], F32)
                    gt_s = alloc(a2, "a2_gt", [128, TB], F32)
                    rs_b = alloc(a2, "a2_rsb", [128, TB], F32)
                    row_r = alloc(a2, "a2_rowr", [1, TB], F32)

                    b_sc = [pbank(a2, f"pA2_sc{i}") for i in range(2)]
                    b_o = [pbank(a2, f"pA2_o{i}") for i in range(2)]
                    b_dkv = [pbank(a2, f"pA2_dkv{i}") for i in range(2)]
                    b_tr1 = pbank(a2, "pA2_tr1", BF16)
                    b_tr2 = pbank(a2, "pA2_tr2", BF16)

                    kvb_t = [0] * NH
                    tr1_c = [0, 0]
                    tr2_c = [0, 0]
                    sc_c = [0, 0]
                    o_c = [0, 0]
                    dkv_c = [0, 0]
                    kts_c = [0, 0]
                    vts_c = [0, 0]
                    qs_c = [0, 0]
                    pts_c = [0, 0]
                    qkvb_pe = [0, 0]
                    qkvb_dve = [0, 0]
                    gateb_c = [0, 0]
                    t_gated = 0
                    t_msq = 0
                    t_racc = 0
                    rowr_act = 0
                    rsb_c = 0
                    st_prev = 0
                    gated_st = []
                    for tb in range(NBLK):
                        sl = tb % 2
                        t0 = tb * TB
                        K.wait("qg", "pe", qkvb_pe[sl])
                        K.wait("qg", "dve", qkvb_dve[sl])
                        qkv_tok = K.dma("qg", qkvb[sl][:, :], qkv_scr[tb][:, :])
                        K.wait("qg", "dve", gateb_c[sl])
                        gate_tok = K.dma("qg", gateb[sl][:, :], gate_scr[tb][:, :])
                        K.wait("pe", "qg", qkv_tok)
                        K.wait("act", "dve", t_gated)
                        for h in range(NH):
                            hp = h % 2
                            qof = (3 * h) * TB
                            kof = (3 * h + 1) * TB
                            vof = (3 * h + 2) * TB
                            tro = hp * 256
                            K.wait("pe", "dve", tr1_c[hp])
                            for c in range(2):
                                K.emit("pe", lambda e, sl=sl, kof=kof, c=c, tro=tro:
                                       e.transpose(
                                           b_tr1[:, tro + c * 128:tro + (c + 1) * 128],
                                           qkvb[sl][:, kof + c * 128:kof + (c + 1) * 128],
                                           ident[:, :]), inc=(c == 1))
                            t_ktr = K.tok("pe")
                            K.wait("pe", "act", tr2_c[hp])
                            for c in range(2):
                                K.emit("pe", lambda e, sl=sl, vof=vof, c=c, tro=tro:
                                       e.transpose(
                                           b_tr2[:, tro + c * 128:tro + (c + 1) * 128],
                                           qkvb[sl][:, vof + c * 128:vof + (c + 1) * 128],
                                           ident[:, :]), inc=(c == 1))
                            t_vtr = K.tok("pe")
                            K.wait("pe", "dve", sc_c[hp])
                            for c in range(2):
                                K.emit("pe", lambda e, sl=sl, kof=kof, qof=qof, c=c,
                                       hp=hp: e.matmul(
                                           b_sc[hp][:, c * 256:(c + 1) * 256],
                                           qkvb[sl][:, kof + c * 128:kof + (c + 1) * 128],
                                           qkvb[sl][:, qof:qof + TB],
                                           start=True, stop=True), inc=(c == 1))
                            t_sc = K.tok("pe")
                            qkvb_pe[sl] = t_sc
                            K.wait("dve", "pe", t_ktr)
                            K.wait("dve", "pe", kts_c[hp])
                            for c in range(2):
                                K.emit("dve", lambda e, c=c, h=h, hp=hp: e.tensor_scalar(
                                    kT_s[hp][:, c * 128:(c + 1) * 128],
                                    b_tr1[:, hp * 256 + c * 128:hp * 256 + (c + 1) * 128],
                                    kdec[:, 2 * h + c:2 * h + c + 1], None, ALU.mult),
                                    inc=(c == 1))
                            tr1_c[hp] = K.tok("dve")
                            K.wait("act", "pe", t_vtr)
                            K.wait("act", "pe", vts_c[hp])
                            for c in range(2):
                                K.emit("act", lambda e, c=c, hp=hp: e.copy(
                                    vT_s[hp][:, c * 128:(c + 1) * 128],
                                    b_tr2[:, hp * 256 + c * 128:hp * 256 + (c + 1) * 128]),
                                    inc=(c == 1))
                            t_vt = K.tok("act")
                            tr2_c[hp] = t_vt
                            K.wait("dve", "pe", qs_c[hp])
                            K.emit("dve", lambda e, sl=sl, qof=qof, h=h, hp=hp:
                                   e.tensor_mul(
                                       qs_s[hp][:, :], qkvb[sl][:, qof:qof + TB],
                                       qdec[:, h * BLK:(h + 1) * BLK]), inc=1)
                            t_qs = K.tok("dve")
                            qkvb_dve[sl] = t_qs
                            K.wait("dve", "pe", t_sc)
                            K.wait("dve", "pe", pts_c[hp])
                            for c in range(2):
                                K.emit("dve", lambda e, c=c, h=h, hp=hp: e.tensor_mul(
                                    pT_s[hp][:, c * TB:(c + 1) * TB],
                                    b_sc[hp][:, c * 256:(c + 1) * 256],
                                    maskT[:, (2 * h + c) * BLK:(2 * h + c + 1) * BLK]),
                                    inc=(c == 1))
                            t_pT = K.tok("dve")
                            sc_c[hp] = t_pT
                            K.wait("pe", "act", o_c[hp])
                            K.wait("pe", "dve", t_qs)
                            K.wait("pe", "act", kvb_t[h])
                            K.emit("pe", lambda e, h=h, hp=hp: e.matmul(
                                b_o[hp][:, 0:256], kvbf[:, h * HD:(h + 1) * HD],
                                qs_s[hp][:, :], start=True, stop=False))
                            K.wait("pe", "act", t_vt)
                            K.wait("pe", "dve", t_pT)
                            for c in range(2):
                                K.emit("pe", lambda e, c=c, hp=hp: e.matmul(
                                    b_o[hp][:, 0:256], vT_s[hp][:, c * 128:(c + 1) * 128],
                                    pT_s[hp][:, c * TB:(c + 1) * TB],
                                    start=False, stop=(c == 1)), inc=(c == 1))
                            t_o = K.tok("pe")
                            qs_c[hp] = t_o
                            pts_c[hp] = t_o
                            K.wait("pe", "dve", dkv_c[hp])
                            for c in range(2):
                                K.emit("pe", lambda e, c=c, hp=hp: e.matmul(
                                    b_dkv[hp][:, 0:128], kT_s[hp][:, c * 128:(c + 1) * 128],
                                    vT_s[hp][:, c * 128:(c + 1) * 128],
                                    start=(c == 0), stop=(c == 1)), inc=(c == 1))
                            t_dkv = K.tok("pe")
                            kts_c[hp] = t_dkv
                            vts_c[hp] = t_dkv
                            K.wait("act", "pe", t_o)
                            K.emit("act", lambda e, h=h, hp=hp: e.copy(
                                attnb[:, h * TB:(h + 1) * TB], b_o[hp][:, 0:256]), inc=1)
                            o_c[hp] = K.tok("act")
                            K.wait("dve", "pe", t_dkv)
                            K.emit("dve", lambda e, h=h, hp=hp: e.scalar_tensor_tensor(
                                kv32[:, h * HD:(h + 1) * HD], kv32[:, h * HD:(h + 1) * HD],
                                s_bd[:, h:h + 1], b_dkv[hp][:, 0:128], ALU.mult, ALU.add),
                                inc=1)
                            dkv_c[hp] = K.tok("dve")
                            K.wait("act", "dve", dkv_c[hp])
                            K.emit("act", lambda e, h=h: e.copy(
                                kvbf[:, h * HD:(h + 1) * HD],
                                kv32[:, h * HD:(h + 1) * HD]), inc=1)
                            kvb_t[h] = K.tok("act")
                        t_attn = K.tok("act")
                        K.wait("act", "dve", t_racc)
                        sqd = {}
                        for hc in range(NHC):
                            ssl = hc % 2
                            K.wait("act", "dve", sqd.get(hc - 2, 0))
                            K.emit("act", lambda e, hc=hc, ssl=ssl: e.activation(
                                sq_s[:, ssl * TB:(ssl + 1) * TB],
                                attnb[:, hc * TB:(hc + 1) * TB], AF.Square), inc=1)
                            K.wait("dve", "act", K.tok("act"))
                            if hc == 0:
                                K.wait("dve", "pe", t_msq)
                                K.emit("dve", lambda e, ssl=ssl: e.tensor_copy(
                                    rms_acc[:, :], sq_s[:, ssl * TB:(ssl + 1) * TB]),
                                    inc=1)
                            else:
                                K.emit("dve", lambda e, ssl=ssl: e.tensor_add(
                                    rms_acc[:, :], rms_acc[:, :],
                                    sq_s[:, ssl * TB:(ssl + 1) * TB]), inc=1)
                            sqd[hc] = K.tok("dve")
                        t_racc = K.tok("dve")
                        K.wait("pe", "dve", t_racc)
                        K.wait("pe", "dve", dkv_c[0])
                        K.emit("pe", lambda e: e.matmul(
                            b_dkv[0][0:1, 128:128 + TB], ones_col_f32[:, :],
                            rms_acc[:, :], start=True, stop=True), inc=1)
                        t_msq = K.tok("pe")
                        K.wait("dve", "pe", t_msq)
                        K.wait("dve", "act", rowr_act)
                        K.emit("dve", lambda e: e.tensor_scalar(
                            row_r[0:1, :], b_dkv[0][0:1, 128:128 + TB], 1.0 / H, 1e-6,
                            ALU.mult, ALU.add), inc=1)
                        K.wait("act", "dve", K.tok("dve"))
                        K.emit("act", lambda e: e.activation(row_r[0:1, :], row_r[0:1, :],
                                                             AF.Ln))
                        K.emit("act", lambda e: e.activation(row_r[0:1, :], row_r[0:1, :],
                                                             AF.Exp, scale=-0.5), inc=1)
                        rowr_act = K.tok("act")
                        K.wait("pe", "act", rowr_act)
                        K.wait("pe", "act", o_c[0])
                        K.wait("pe", "dve", rsb_c)
                        K.emit("pe", lambda e: e.matmul(
                            b_o[0][:, 256:512], ones_row_f32[:, :], row_r[0:1, :],
                            start=True, stop=True), inc=1)
                        K.wait("dve", "pe", K.tok("pe"))
                        K.emit("dve", lambda e: e.tensor_copy(rs_b[:, :],
                                                              b_o[0][:, 256:512]),
                               inc=1)
                        rsb_c = K.tok("dve")
                        K.wait("dve", "qg", gate_tok)
                        K.wait("dve", "act", t_attn)
                        K.wait("dve", "qs", st_prev)
                        for hc in range(NHC):
                            K.emit("dve", lambda e, hc=hc: e.scalar_tensor_tensor(
                                gt_s[:, :], attnb[:, hc * TB:(hc + 1) * TB],
                                s_normw[:, hc:hc + 1], rs_b[:, :], ALU.mult, ALU.mult))
                            K.emit("dve", lambda e, hc=hc, sl=sl: e.tensor_mul(
                                gatedb[:, hc * TB:(hc + 1) * TB], gt_s[:, :],
                                gateb[sl][:, hc * TB:(hc + 1) * TB]),
                                inc=(1 if hc == NHC - 1 else 0))
                        t_gated = K.tok("dve")
                        gateb_c[sl] = t_gated
                        K.wait("qs", "dve", t_gated)
                        st_prev = K.dma("qs", gated_scr[tb][:, :], gatedb[:, :])
                        gated_st.append(st_prev)

                K.barrier()
                with ExitStack() as a2b:
                    TB = BLK
                    wout_all = alloc(a2b, "a2b_wo", [128, NHC * H], BF16)
                    gb = [alloc(a2b, f"a2b_g{i}", [128, NHC * TB], BF16)
                          for i in range(2)]
                    xq = [alloc(a2b, f"a2b_x{i}", [128, NHC * TB], F32)
                          for i in range(2)]
                    yb = [alloc(a2b, f"a2b_y{i}", [128, TB], F32) for i in range(2)]
                    b_y = [pbank(a2b, f"pA2b_{i}") for i in range(3)]

                    for fc in range(NHC):
                        K.dma("qg", wout_all[:, fc * H:(fc + 1) * H], woutT[fc][:, :],
                              ckpt=(fc == NHC - 1))
                    wo_tok = K.tok("qg")
                    K.wait("pe", "dve", K.tok("dve"))
                    K.wait("pe", "act", K.tok("act"))
                    gb_c = [0, 0]
                    yb_st = [0, 0]
                    ybank_c = [0, 0, 0]
                    xq_c = [0, 0]
                    rr2 = 0
                    for tb in range(NBLK):
                        s2 = tb % 2
                        K.wait("qg", "pe", gb_c[s2])
                        K.wait("qg", "qs", gated_st[tb])
                        g_tok = K.dma("qg", gb[s2][:, :], gated_scr[tb][:, :])
                        K.wait("qs", "dve", xq_c[s2])
                        x_tok = load_x_chunk(xT, tb * TB, xq[s2], TB)
                        K.wait("pe", "qg", g_tok)
                        if tb == 0:
                            K.wait("pe", "qg", wo_tok)
                        for fc in range(NHC):
                            bi = rr2 % 3
                            rr2 += 1
                            bk = b_y[bi]
                            K.wait("pe", "dve", ybank_c[bi])
                            for oc in range(NHC):
                                K.emit("pe", lambda e, fc=fc, oc=oc, bk=bk, s2=s2:
                                       e.matmul(
                                           bk[:, 0:TB],
                                           wout_all[:, fc * H + oc * 128:
                                                    fc * H + (oc + 1) * 128],
                                           gb[s2][:, oc * TB:(oc + 1) * TB],
                                           start=(oc == 0), stop=(oc == NHC - 1)),
                                       inc=(1 if oc == NHC - 1 else 0))
                            t_y = K.tok("pe")
                            ysl = fc % 2
                            K.wait("dve", "pe", t_y)
                            K.wait("dve", "qs", x_tok)
                            K.wait("dve", "qs", yb_st[ysl])
                            K.emit("dve", lambda e, fc=fc, bk=bk, ysl=ysl, s2=s2:
                                   e.tensor_add(
                                       yb[ysl][:, :], bk[:, 0:TB],
                                       xq[s2][:, fc * TB:(fc + 1) * TB]), inc=1)
                            ybank_c[bi] = K.tok("dve")
                            xq_c[s2] = ybank_c[bi]
                            K.wait("qs", "dve", ybank_c[bi])
                            yb_st[ysl] = K.dma(
                                "qs", x_res[fc][:, tb * TB:(tb + 1) * TB],
                                yb[ysl][:, :])
                        gb_c[s2] = K.tok("pe")
                    K.barrier()

        if n_sub >= 1:
            phase_a()


        def ffn_phase(tag, src, dst):
            GRP = 8
            NG = NIC // GRP
            NCH = S // TW
            with ExitStack() as fb:
                xb2 = [alloc(fb, f"{tag}_xb{i}", [128, NHC * TW], F32) for i in range(2)]
                xnf2 = [alloc(fb, f"{tag}_xn{i}", [128, NHC * TW], BF16) for i in range(2)]
                scr = ln_scratch(fb, tag)
                wbuf = [alloc(fb, f"{tag}_w{i}", [128, 2 * H], BF16) for i in range(2)]
                wd_grp = alloc(fb, f"{tag}_wd", [128, GRP * H], BF16)
                act_g = [alloc(fb, f"{tag}_ag{i}", [128, GRP * TW], BF16) for i in range(2)]
                silu_s = alloc(fb, f"{tag}_silu", [128, TW], F32)
                b_g = pbank(fb, f"p{tag}_g")
                b_u = pbank(fb, f"p{tag}_u")
                b_d = [pbank(fb, f"p{tag}_d{i}") for i in range(2)]
                bmu = pbank(fb, f"p{tag}_mu")
                brs = pbank(fb, f"p{tag}_rs")
                br1 = pbank(fb, f"p{tag}_r1")
                br2 = pbank(fb, f"p{tag}_r2")

                x_tok = [None] * (NCH + 1)
                ln_tok = [None] * (NCH + 1)
                x_tok[0] = load_x_chunk(src, 0, xb2[0])
                ln_tok[0] = emit_ln(
                    x_tok[0], xb2[0],
                    lambda hc: xnf2[0][:, hc * TW:(hc + 1) * TW],
                    scr, bmu, brs, br1, br2)
                w_tok = [0, 0]
                pe_w = [0, 0]
                t_silu = 0
                t_actg = [0, 0]
                pe_down = 0
                dve_down = [0, 0]
                for tb in range(NCH):
                    sl = tb % 2
                    xb = xb2[sl]
                    xnf = xnf2[sl]
                    t0 = tb * TW
                    if tb + 1 < NCH:
                        x_tok[tb + 1] = load_x_chunk(src, t0 + TW, xb2[1 - sl])
                    K.wait("pe", "dve", ln_tok[tb])
                    for g in range(NG):
                        asl = g % 2
                        K.wait("pe", "dve", t_actg[asl])
                        for j in range(GRP):
                            ib = GRP * g + j
                            wsl = ib % 2
                            K.wait("qg", "pe", pe_w[wsl])
                            w_tok[wsl] = K.dma(
                                "qg", wbuf[wsl].ap().rearrange("p (k f) -> p k f", k=2),
                                ffn8[ib].rearrange("k p f -> p k f"))
                            K.wait("qg", "pe", pe_down)
                            wd_tok = K.dma("qg", wd_grp[:, j * H:(j + 1) * H],
                                           ffn8d[ib][:, :])
                            K.wait("pe", "qg", w_tok[wsl])
                            K.wait("pe", "act", t_silu)
                            for hc in range(NHC):
                                K.emit("pe", lambda e, wsl=wsl, hc=hc, xnf=xnf: e.matmul(
                                    b_g[:, :], wbuf[wsl][:, hc * 128:(hc + 1) * 128],
                                    xnf[:, hc * TW:(hc + 1) * TW],
                                    start=(hc == 0), stop=(hc == NHC - 1)),
                                    inc=(1 if hc == NHC - 1 else 0))
                            t_gmm = K.tok("pe")
                            K.wait("pe", "dve", K.tok("dve"))
                            for hc in range(NHC):
                                K.emit("pe", lambda e, wsl=wsl, hc=hc, xnf=xnf: e.matmul(
                                    b_u[:, :],
                                    wbuf[wsl][:, H + hc * 128:H + (hc + 1) * 128],
                                    xnf[:, hc * TW:(hc + 1) * TW],
                                    start=(hc == 0), stop=(hc == NHC - 1)),
                                    inc=(1 if hc == NHC - 1 else 0))
                            t_umm = K.tok("pe")
                            pe_w[wsl] = t_umm
                            K.wait("act", "pe", t_gmm)
                            K.wait("act", "dve", K.tok("dve"))
                            K.emit("act", lambda e, ib=ib: e.activation(
                                silu_s[:, :], b_g[:, :], AF.Silu,
                                scale=s_gs[:, ib:ib + 1]), inc=1)
                            t_silu = K.tok("act")
                            K.wait("dve", "act", t_silu)
                            K.wait("dve", "pe", t_umm)
                            K.emit("dve", lambda e, ib=ib, asl=asl, j=j:
                                   e.scalar_tensor_tensor(
                                       act_g[asl][:, j * TW:(j + 1) * TW],
                                       b_u[:, :], s_us[:, ib:ib + 1], silu_s[:, :],
                                       ALU.mult, ALU.mult), inc=1)
                        t_actg[asl] = K.tok("dve")
                        K.wait("pe", "qg", wd_tok)
                        K.wait("pe", "dve", t_actg[asl])
                        for oc in range(NHC):
                            dsl = oc % 2
                            K.wait("pe", "dve", dve_down[dsl])
                            for j in range(GRP):
                                K.emit("pe", lambda e, j=j, oc=oc, dsl=dsl, asl=asl:
                                       e.matmul(
                                           b_d[dsl][:, :],
                                           wd_grp[:, j * H + oc * 128:j * H + (oc + 1) * 128],
                                           act_g[asl][:, j * TW:(j + 1) * TW],
                                           start=(j == 0), stop=(j == GRP - 1)),
                                       inc=(1 if j == GRP - 1 else 0))
                            t_dmm = K.tok("pe")
                            K.wait("dve", "pe", t_dmm)
                            K.emit("dve", lambda e, oc=oc, dsl=dsl, xb=xb:
                                   e.scalar_tensor_tensor(
                                       xb[:, oc * TW:(oc + 1) * TW], b_d[dsl][:, :],
                                       s_ds[:, oc:oc + 1], xb[:, oc * TW:(oc + 1) * TW],
                                       ALU.mult, ALU.add), inc=1)
                            dve_down[dsl] = K.tok("dve")
                        pe_down = K.tok("pe")
                        if g == 1 and tb + 1 < NCH:
                            ln_tok[tb + 1] = emit_ln(
                                x_tok[tb + 1], xb2[1 - sl],
                                lambda hc, o=1 - sl: xnf2[o][:, hc * TW:(hc + 1) * TW],
                                scr, bmu, brs, br1, br2)
                    K.wait("qs", "dve", K.tok("dve"))
                    store_x_chunk(dst, t0, xb)
                K.barrier()

        if n_sub >= 2:
            ffn_phase("fb", x_res, x_res)

        def phase_c():
                with ExitStack() as phc:
                    if DEBUG_TAPS:
                        dbg_xres = nc.dram_tensor("dbg_xres", [NHC, 128, S], F32,
                                                  kind="ExternalOutput").ap()
                        K.barrier()
                        K.dma("qs", dbg_xres[:, :, :], x_res[:, :, :])
                        K.wait("qs", "qs", K.tok("qs"))
                    K_rot = alloc(phc, "K_rot", [128, S], BF16)
                    V_tm = alloc(phc, "V_tm", [128, S], BF16)

                    with ExitStack() as c1:
                        xb = alloc(c1, "c1_xb", [128, NHC * TW], F32)
                        xn = alloc(c1, "c1_xn", [128, NHC * S], BF16)
                        scr = ln_scratch(c1, "c1")
                        rot = {nm: alloc(c1, f"c1_{nm}", [128, S], F32)
                               for nm in ("rot_cq", "rot_sq", "rot_ck", "rot_sk")}
                        wb = [alloc(c1, f"c1_w{i}", [128, H], BF16) for i in range(2)]
                        tc_s = alloc(c1, "c1_tc", [128, TW], F32)
                        ts_s = alloc(c1, "c1_ts", [128, TW], F32)
                        qr_s = alloc(c1, "c1_qr", [128, TW], BF16)
                        V_fm = alloc(c1, "c1_vfm", [128, S], BF16)
                        pq = [pbank(c1, f"pC1_{i}") for i in range(2)]
                        b_tr = pbank(c1, "pC1_tr", BF16)
                        bmu = pbank(c1, "pC1_mu")
                        brs = pbank(c1, "pC1_rs")
                        br1 = pbank(c1, "pC1_r1")
                        br2 = pbank(c1, "pC1_r2")

                        for nm, t in rot.items():
                            K.dma("qs", t[:, :], dt_in[nm][:, :])
                        rot_tok = K.tok("qs")
                        ln_prev = 0
                        for tc in range(S // TW):
                            t0 = tc * TW
                            K.wait("qs", "dve", ln_prev)
                            tld = load_x_chunk(x_res, t0, xb)
                            if DEBUG_TAPS and tc == 0:
                                dbg_xb = nc.dram_tensor("dbg_xb", [128, NHC * TW], F32,
                                                        kind="ExternalOutput").ap()
                                K.wait("qs", "qs", tld)
                                K.dma("qs", dbg_xb[:, :], xb[:, :])
                            ln_prev = emit_ln(
                                tld, xb,
                                lambda hc, t0=t0: xn[:, hc * S + t0:hc * S + t0 + TW],
                                scr, bmu, brs, br1, br2)
                        if DEBUG_TAPS:
                            dbg_xn = nc.dram_tensor("dbg_xn", [128, NHC * S], BF16,
                                                    kind="ExternalOutput").ap()
                            K.dma("qs", dbg_xn[:, :], xn[:, :])
                            K.wait("qs", "qs", K.tok("qs"))
                        K.wait("dve", "qs", rot_tok)

                        def rotary_evac(p_src, cosT, sinT, t0, dst_ap, tok_inc):
                            K.emit("dve", lambda e: e.tensor_mul(
                                tc_s[:, :], p_src[:, :], cosT[:, t0:t0 + TW]))
                            K.emit("dve", lambda e: e.tensor_mul(
                                ts_s[0:64, :], p_src[64:128, :], sinT[0:64, t0:t0 + TW]))
                            K.emit("dve", lambda e: e.tensor_mul(
                                ts_s[64:128, :], p_src[0:64, :], sinT[64:128, t0:t0 + TW]))
                            K.emit("dve", lambda e: e.tensor_add(
                                dst_ap, tc_s[:, :], ts_s[:, :]), inc=tok_inc)

                        w_tok = [0, 0]
                        pe_w = [0, 0]
                        ev_p = [0, 0]
                        st_q = 0
                        K.wait("pe", "dve", K.tok("dve"))
                        for j in range(18):
                            is_kv = j >= 16
                            oc = j - 16 if is_kv else j
                            sl = j % 2
                            src = (wkvT if is_kv else wqT)[oc]
                            K.wait("qg", "pe", pe_w[sl])
                            w_tok[sl] = K.dma("qg", wb[sl][:, :], src[:, :])
                            K.wait("pe", "qg", w_tok[sl])
                            for q in range(4):
                                psl = q % 2
                                K.wait("pe", "dve", ev_p[psl])
                                for hc in range(NHC):
                                    K.emit("pe", lambda e, sl=sl, q=q, hc=hc, psl=psl:
                                           e.matmul(
                                               pq[psl][:, :],
                                               wb[sl][:, hc * 128:(hc + 1) * 128],
                                               xn[:, hc * S + q * 512:hc * S + (q + 1) * 512],
                                               start=(hc == 0), stop=(hc == NHC - 1)),
                                           inc=(1 if hc == NHC - 1 else 0))
                                if q == 3:
                                    pe_w[sl] = K.tok("pe")
                                t_mm = K.tok("pe")
                                K.wait("dve", "pe", t_mm)
                                t0 = q * TW
                                if not is_kv:
                                    K.wait("dve", "qs", st_q)
                                    rotary_evac(pq[psl], rot["rot_cq"], rot["rot_sq"], t0,
                                                qr_s[:, :], 1)
                                    ev_p[psl] = K.tok("dve")
                                    K.wait("qs", "dve", ev_p[psl])
                                    st_q = K.dma(
                                        "qs", q_scr[q, :, oc * TW:(oc + 1) * TW],
                                        qr_s[:, :])
                                elif oc == 0:
                                    rotary_evac(pq[psl], rot["rot_ck"], rot["rot_sk"], t0,
                                                K_rot[:, t0:t0 + TW], 1)
                                    ev_p[psl] = K.tok("dve")
                                else:
                                    K.emit("dve", lambda e, t0=t0, psl=psl: e.tensor_copy(
                                        V_fm[:, t0:t0 + TW], pq[psl][:, :]), inc=1)
                                    ev_p[psl] = K.tok("dve")
                        if DEBUG_TAPS:
                            dbg_kr = nc.dram_tensor("dbg_kr", [128, S], BF16,
                                                    kind="ExternalOutput").ap()
                            K.wait("qs", "dve", K.tok("dve"))
                            K.dma("qs", dbg_kr[:, :], K_rot[:, :])
                        K.wait("pe", "dve", K.tok("dve"))
                        for c in range(NHC):
                            K.emit("pe", lambda e, c=c: e.transpose(
                                b_tr[:, (c % 4) * 128:(c % 4 + 1) * 128],
                                V_fm[:, c * 128:(c + 1) * 128], ident[:, :]), inc=1)
                            K.wait("dve", "pe", K.tok("pe"))
                            K.emit("dve", lambda e, c=c: e.tensor_copy(
                                V_tm[:, c * 128:(c + 1) * 128],
                                b_tr[:, (c % 4) * 128:(c % 4 + 1) * 128]), inc=1)
                            K.wait("pe", "dve", K.tok("dve"))
                        if DEBUG_TAPS:
                            dbg_vt = nc.dram_tensor("dbg_vt", [128, S], BF16,
                                                    kind="ExternalOutput").ap()
                            K.wait("qs", "dve", K.tok("dve"))
                            K.dma("qs", dbg_vt[:, :], V_tm[:, :])
                        K.barrier()

                    with ExitStack() as c2:
                        qb = alloc(c2, "c2_q", [128, NH * TW], BF16)
                        x_in = alloc(c2, "c2_xin", [128, NHC * TW], F32)
                        xnew = alloc(c2, "c2_xnew", [128, NHC * TW], F32)
                        attn_o = alloc(c2, "c2_attn", [128, NH * TW], BF16)
                        wpb = alloc(c2, "c2_wp", [128, NHC * H], BF16)
                        rs_row = alloc(c2, "c2_rsr", [1, 2 * TW], F32)
                        rsb_s = alloc(c2, "c2_rsb", [128, TW], F32)
                        accd = [alloc(c2, f"c2_acc{i}", [128, TW], F32) for i in range(2)]
                        pr_s = [alloc(c2, f"c2_pr{i}", [128, TW], BF16) for i in range(4)]

                        b_mm = [pbank(c2, f"pC2_m{i}") for i in range(3)]
                        b_out = [pbank(c2, f"pC2_o{i}") for i in range(2)]
                        b_sum = [pbank(c2, f"pC2_s{i}") for i in range(2)]

                        for fc in range(NHC):
                            K.dma("qg", wpb[:, fc * H:(fc + 1) * H], wprojT[fc][:, :],
                                  ckpt=(fc == NHC - 1))
                        wp_tok = K.tok("qg")

                        mmc = [("dve", 0)] * 3
                        prc = [0, 0, 0, 0]
                        prd = [0, 0, 0, 0]
                        outc = [0, 0]
                        sumc = [0, 0]
                        accc = [0, 0]
                        rr = 0

                        def _flush_pv(p):
                            prb, pi, jt, njt, hp, gate = p
                            K.wait("pe", *gate)
                            if jt == 0:
                                K.wait("pe", "dve", outc[hp])
                            K.emit("pe", lambda e, prb=prb, jt=jt, njt=njt, hp=hp:
                                   e.matmul(
                                       b_out[hp][:, :],
                                       V_tm[:, jt * 128:(jt + 1) * 128], prb[:, :],
                                       start=(jt == 0), stop=(jt == njt - 1)), inc=1)
                            prc[pi] = K.tok("pe")
                        for tb in range(S // TW):
                            t0 = tb * TW
                            q_tok = K.dma("qg", qb[:, :], q_scr[tb][:, :])
                            x_tok = load_x_chunk(x_res, t0, x_in)
                            K.wait("pe", "qg", q_tok)
                            for h in range(NH):
                                hp = h % 2
                                njt = (t0 + TW) // 128
                                pend = []
                                for jt in range(njt):
                                    bi = rr % 3
                                    rr += 1
                                    bk = b_mm[bi]
                                    K.wait("pe", *mmc[bi])
                                    K.emit("pe", lambda e, h=h, jt=jt, bk=bk: e.matmul(
                                        bk[:, :], K_rot[:, jt * 128:(jt + 1) * 128],
                                        qb[:, h * TW:(h + 1) * TW],
                                        start=True, stop=True), inc=1)
                                    t_mm = K.tok("pe")
                                    pi = jt % 4
                                    prb = pr_s[pi]
                                    K.wait("act", "pe", t_mm)
                                    K.wait("act", "pe", prc[pi])
                                    K.wait("act", "dve", prd[pi])
                                    K.emit("act", lambda e, prb=prb, bk=bk: e.activation(
                                        prb[:, :], bk[:, :], AF.Exp), inc=1)
                                    t_exp = K.tok("act")
                                    mmc[bi] = ("act", t_exp)
                                    K.wait("dve", "act", t_exp)
                                    kdiag = jt - 4 * tb
                                    if kdiag >= 0:
                                        K.emit("dve", lambda e, prb=prb, kdiag=kdiag:
                                               e.tensor_mul(
                                                   prb[:, :], prb[:, :],
                                                   s_m01[:, kdiag * TW:(kdiag + 1) * TW]),
                                               inc=1)
                                        pv_gate = ("dve", K.tok("dve"))
                                    else:
                                        pv_gate = ("act", t_exp)
                                    if jt == 0:
                                        K.wait("dve", "pe", accc[hp])
                                        K.emit("dve", lambda e, prb=prb, hp=hp:
                                               e.tensor_copy(accd[hp][:, :], prb[:, :]),
                                               inc=1)
                                    else:
                                        K.emit("dve", lambda e, prb=prb, hp=hp:
                                               e.tensor_add(accd[hp][:, :],
                                                            accd[hp][:, :], prb[:, :]),
                                               inc=1)
                                    prd[pi] = K.tok("dve")
                                    if len(pend) >= 2:
                                        _flush_pv(pend.pop(0))
                                    pend.append((prb, pi, jt, njt, hp, pv_gate))
                                for p in pend:
                                    _flush_pv(p)
                                t_acc = K.tok("dve")
                                K.wait("pe", "dve", t_acc)
                                K.wait("pe", "dve", sumc[hp])
                                K.emit("pe", lambda e, hp=hp: e.matmul(
                                    b_sum[hp][0:1, :], ones_col_f32[:, :],
                                    accd[hp][:, :], start=True, stop=True), inc=1)
                                accc[hp] = K.tok("pe")
                                K.wait("dve", "pe", accc[hp])
                                K.emit("dve", lambda e, hp=hp: e.reciprocal(
                                    rs_row[0:1, hp * TW:(hp + 1) * TW],
                                    b_sum[hp][0:1, :]), inc=1)
                                sumc[hp] = K.tok("dve")
                                bi = rr % 3
                                rr += 1
                                bk = b_mm[bi]
                                K.wait("pe", *mmc[bi])
                                K.wait("pe", "dve", sumc[hp])
                                K.emit("pe", lambda e, hp=hp, bk=bk: e.matmul(
                                    bk[:, :], ones_row_f32[:, :],
                                    rs_row[0:1, hp * TW:(hp + 1) * TW],
                                    start=True, stop=True), inc=1)
                                K.wait("dve", "pe", K.tok("pe"))
                                K.emit("dve", lambda e, bk=bk: e.tensor_copy(
                                    rsb_s[:, :], bk[:, :]))
                                K.emit("dve", lambda e, h=h, hp=hp: e.tensor_mul(
                                    attn_o[:, h * TW:(h + 1) * TW], b_out[hp][:, :],
                                    rsb_s[:, :]), inc=1)
                                outc[hp] = K.tok("dve")
                                mmc[bi] = ("dve", outc[hp])
                            t_attn = K.tok("dve")
                            K.wait("pe", "dve", t_attn)
                            K.wait("pe", "qg", wp_tok)
                            K.wait("dve", "qs", x_tok)
                            for fc in range(NHC):
                                bi = rr % 3
                                rr += 1
                                bk = b_mm[bi]
                                K.wait("pe", *mmc[bi])
                                for oc in range(NHC):
                                    K.emit("pe", lambda e, fc=fc, oc=oc, bk=bk: e.matmul(
                                        bk[:, :],
                                        wpb[:, fc * H + oc * 128:fc * H + (oc + 1) * 128],
                                        attn_o[:, oc * TW:(oc + 1) * TW],
                                        start=(oc == 0), stop=(oc == NHC - 1)),
                                        inc=(1 if oc == NHC - 1 else 0))
                                t_y = K.tok("pe")
                                K.wait("dve", "pe", t_y)
                                K.emit("dve", lambda e, fc=fc, bk=bk:
                                       e.scalar_tensor_tensor(
                                           xnew[:, fc * TW:(fc + 1) * TW], bk[:, :],
                                           s_bproj[:, fc:fc + 1],
                                           x_in[:, fc * TW:(fc + 1) * TW],
                                           ALU.add, ALU.add), inc=1)
                                mmc[bi] = ("dve", K.tok("dve"))
                            K.wait("qs", "dve", K.tok("dve"))
                            store_x_chunk(x_res, t0, xnew)
                            K.barrier()
                            mmc = [("dve", 0)] * 3
                            prc = [0, 0, 0, 0]
                            prd = [0, 0, 0, 0]
                            outc = [0, 0]
                            sumc = [0, 0]
                            accc = [0, 0]

        if n_sub >= 3:
            phase_c()


        if n_sub >= 4:
            ffn_phase("fd", x_res, yT)
            fin = K.tok("qs")
        else:
            K.barrier()
            fin = K.dma("qs", yT[:, :, :], x_res[:, :, :])
        if DEBUG_TAPS:
            dbg_qkv = nc.dram_tensor("dbg_qkv", [NBLK, 48, 128, BLK], BF16,
                                     kind="ExternalOutput").ap()
            dbg_gate = nc.dram_tensor("dbg_gate", [NBLK, 16, 128, BLK], BF16,
                                      kind="ExternalOutput").ap()
            K.barrier()
            K.dma("qs", dbg_qkv[:, :, :, :], qkv_scr[:, :, :, :])
            fin = K.dma("qs", dbg_gate[:, :, :, :], gate_scr[:, :, :, :])
            if n_sub >= 3:
                dbg_q = nc.dram_tensor("dbg_q", [S // TW, 16, 128, TW], BF16,
                                       kind="ExternalOutput").ap()
                fin = K.dma("qs", dbg_q[:, :, :, :], q_scr[:, :, :, :])
        K.wait("qs", "qs", fin)
        K.flush()

    return nc



def _prep_shared(inputs):
    sh = {}
    sh["wqkvT"] = _tile_lhsT_oc(np.asarray(inputs["L0_attn_a_w_qkv"], dtype=np.float32))
    sh["wgateT"] = _tile_lhsT_oc(np.asarray(inputs["L0_attn_a_w_gate"], dtype=np.float32))
    sh["woutT"] = _tile_lhsT_oc(np.asarray(inputs["L0_attn_a_w_out"], dtype=np.float32))
    sh["wqT"] = _tile_lhsT_oc(np.asarray(inputs["L0_attn_b_w_q"], dtype=np.float32))
    sh["wkvT"] = _tile_lhsT_oc(np.asarray(inputs["L0_attn_b_w_kv"], dtype=np.float32))
    sh["wprojT"] = _tile_lhsT_oc(np.asarray(inputs["L0_attn_b_w_proj"], dtype=np.float32))
    sh["slope"] = np.ascontiguousarray(
        np.asarray(inputs["L0_attn_a_slope_rate"], dtype=np.float32).reshape(1, NH))
    sh["normw"] = np.ascontiguousarray(
        np.asarray(inputs["L0_attn_a_norm_w"], dtype=np.float32).reshape(NHC, 128).T)
    sh["bproj"] = np.ascontiguousarray(
        np.asarray(inputs["L0_attn_b_b_proj"], dtype=np.float32).reshape(NHC, 128).T)

    def enc(w):
        u8 = np.asarray(w).astype(np.uint8)
        return (u8 ^ np.uint8(0x80)).view(np.int8)

    g8 = _tile_lhsT_oc(enc(inputs["L0_ffn_gate_w"]))
    u8_ = _tile_lhsT_oc(enc(inputs["L0_ffn_up_w"]))
    d8 = _tile_lhsT_hc(enc(inputs["L0_ffn_down_w"]))
    sh["ffn8"] = np.ascontiguousarray(np.stack([g8, u8_], axis=1))
    sh["ffn8d"] = np.ascontiguousarray(d8)
    sh["gs_t"] = np.ascontiguousarray(
        np.asarray(inputs["L0_ffn_gate_scale"], dtype=np.float32).reshape(NIC, 128).T)
    sh["us_t"] = np.ascontiguousarray(
        np.asarray(inputs["L0_ffn_up_scale"], dtype=np.float32).reshape(NIC, 128).T)
    sh["ds_t"] = np.ascontiguousarray(
        np.asarray(inputs["L0_ffn_down_scale"], dtype=np.float32).reshape(NHC, 128).T)
    for nm, arr in _host_constants().items():
        sh[nm] = arr
    for nm, arr in _rotary_tables().items():
        sh[nm] = arr
    return sh


def kernel(**inputs):
    x = np.asarray(inputs["L0_attn_a_hidden_states"], dtype=np.float32)
    sh = _prep_shared(inputs)
    nc = build_program(N_SUB)
    in_maps = []
    for c in range(B):
        m = dict(sh)
        m["xT"] = np.ascontiguousarray(x[c].T).reshape(NHC, 128, S)
        in_maps.append(m)
    res = run_bass_kernel_spmd(nc, in_maps, core_ids=list(range(B)))
    globals()["_last_res"] = res
    out = np.empty((B, S, H), dtype=np.float32)
    for c in range(B):
        out[c] = res.results[c]["yT"].reshape(H, S).T
    return out


if __name__ == "__main__":
    import reference as _r
    inp = _r.setup_inputs()
    out = kernel(**{k: np.asarray(v) for k, v in inp.items()})
    print("out", out.shape, out.dtype, "finite:", np.isfinite(out).all())
